# version 06
# baseline (speedup 1.0000x reference)
import contextlib
import numpy as np
import ml_dtypes
import concourse.bass as bass
import concourse.mybir as mybir
from concourse.bass_utils import run_bass_kernel_spmd

F32 = mybir.dt.float32
BF16 = mybir.dt.bfloat16
I32 = mybir.dt.int32
AF = mybir.ActivationFunctionType
OP = mybir.AluOpType
AX = mybir.AxisListType
BF = ml_dtypes.bfloat16

D = 1024
FF = 4096
NDT = D // 128
NFT = FF // 128
EPS = 1e-5


def build(NB, T, dbg=False):
    NKB = NB // 128
    NSC = NB // 512
    NQC = T // 512
    NTT = T // 128
    NFC = T // 256
    FCW = 256
    assert T % 512 == 0 and NB % 512 == 0 and NSC <= 16 and NFC <= 8

    nc = bass.Bass("TRN2", target_bir_lowering=False,
                   detect_race_conditions=False)

    xT_h = nc.dram_tensor("xT", [D, NB], F32, kind="ExternalInput").ap()
    xow_h = nc.dram_tensor("xown", [T, D], F32, kind="ExternalInput").ap()
    wq_h = nc.dram_tensor("wq", [128, NDT * 512], F32, kind="ExternalInput").ap()
    wkv_h = nc.dram_tensor("wkv", [128, NDT * 128], F32, kind="ExternalInput").ap()
    wo_h = nc.dram_tensor("wo", [128, 4 * D], F32, kind="ExternalInput").ap()
    w1t_h = nc.dram_tensor("w1t", [128, NDT * FF], F32, kind="ExternalInput").ap()
    w2t_h = nc.dram_tensor("w2t", [128, NFT * D], F32, kind="ExternalInput").ap()
    cw_h = nc.dram_tensor("convw", [128, 3 * NFT], F32, kind="ExternalInput").ap()
    cb_h = nc.dram_tensor("convb", [128, NFT], F32, kind="ExternalInput").ap()
    nw_h = nc.dram_tensor("nw", [128, NDT], F32, kind="ExternalInput").ap()
    nb_h = nc.dram_tensor("nb", [128, NDT], F32, kind="ExternalInput").ap()
    nwb_h = nc.dram_tensor("nwb", [128, D], F32, kind="ExternalInput").ap()
    nbb_h = nc.dram_tensor("nbb", [128, D], F32, kind="ExternalInput").ap()
    onw_h = nc.dram_tensor("onwb", [128, D], F32, kind="ExternalInput").ap()
    onb_h = nc.dram_tensor("onbb", [128, D], F32, kind="ExternalInput").ap()
    hm_h = nc.dram_tensor("hmask", [128, 2], F32, kind="ExternalInput").ap()
    y_h = nc.dram_tensor("y", [T, D], F32, kind="ExternalOutput").ap()
    scr_h = nc.dram_tensor("scr", [4 * NB + 8192], F32, kind="Internal").ap()
    spill_h = nc.dram_tensor("spill", [T, D], F32, kind="Internal").ap()
    w2qh = nc.dram_tensor("w2qh", [128, NFT * D], BF16, kind="Internal").ap()
    if dbg:
        dbg_xnq = nc.dram_tensor("dbg_xnq", [128, NDT * (T + 2)], BF16, kind="ExternalOutput").ap()
        dbg_qtd = nc.dram_tensor("dbg_qtd", [128, 8 * T], BF16, kind="ExternalOutput").ap()
        dbg_kt = nc.dram_tensor("dbg_kt", [128, NB], BF16, kind="ExternalOutput").ap()
        dbg_va = nc.dram_tensor("dbg_va", [128, 65 * NKB], BF16, kind="ExternalOutput").ap()
        dbg_iq = nc.dram_tensor("dbg_iq", [128, T + 2], BF16, kind="ExternalOutput").ap()
        dbg_sl = nc.dram_tensor("dbg_sl", [128, NFT * 256], BF16, kind="ExternalOutput").ap()
        dbg_w1q = nc.dram_tensor("dbg_w1q", [128, NDT * FF], BF16, kind="ExternalOutput").ap()
        dbg_exp = nc.dram_tensor("dbg_exp", [128, 2048], BF16, kind="ExternalOutput").ap()
        dbg_s2 = nc.dram_tensor("dbg_s2", [128, 256], F32, kind="ExternalOutput").ap()
        dbg_hq = nc.dram_tensor("dbg_hq", [128, 512], BF16, kind="ExternalOutput").ap()
        dbg_fd = nc.dram_tensor("dbg_fd", [128, 1024], F32, kind="ExternalOutput").ap()
        dbg_cb = nc.dram_tensor("dbg_cb", [128, 224], F32, kind="ExternalOutput").ap()
        dbg_al = nc.dram_tensor("dbg_al", [128, 1024], F32, kind="ExternalOutput").ap()
        dbg_xn2b = nc.dram_tensor("dbg_xn2b", [128, NDT * 512], BF16, kind="ExternalOutput").ap()
        dbg_scst = nc.dram_tensor("dbg_scst", [1, 320], F32, kind="ExternalOutput").ap()
        dbg_cb2 = nc.dram_tensor("dbg_cb2", [128, 224], F32, kind="ExternalOutput").ap()

    es = contextlib.ExitStack()

    def sb(name, shape, dt=F32):
        return es.enter_context(nc.sbuf_tensor(name, shape, dt))

    with es:
        xnq = sb("xnq", [128, NDT * (T + 2)], BF16)
        invqsb = sb("invqsb", [128, T + 2], BF16)
        cwc = sb("cwc", [128, 3 * NFT])
        cbc = sb("cbc", [128, NFT])
        nwc = sb("nwc", [128, NDT])
        nbc = sb("nbc", [128, NDT])
        hmask = sb("hmask_s", [128, 2])
        ones_r = sb("ones_r", [1, 128], BF16)
        ones_c = sb("ones_c", [128, 1], BF16)
        fatb = sb("fatb", [16, 2 * 512], BF16)
        colb = sb("colb", [128, 224])
        colbb = sb("colbb", [128, 16], BF16)
        scst = sb("scst", [1, 320])
        scstb = sb("scstb", [1, 16], BF16)
        halos = sb("halos", [128, 40])
        rowt = sb("rowt", [128, 128], BF16)
        colpad = sb("colpad", [128, 128], BF16)

        CB_VB, CB_U, CB_NEGU, CB_M1BAR, CB_M2B127 = 0, 1, 2, 3, 4
        CB_M1, CB_A1, CB_A, CB_G, CB_MX = 16, 32, 48, 64, 80
        CB_T0, CB_T1, CB_IQ = 96, 112, 128
        CB_S1, CB_S2, CB_LNA, CB_LNB = 144, 160, 176, 192
        CB_G2W = 208

        pacc = es.enter_context(nc.psum_tensor("pacc", [128, 2048], F32))

        s_in = es.enter_context(nc.semaphore("s_in"))
        s_i2 = es.enter_context(nc.semaphore("s_i2"))
        s_gx = es.enter_context(nc.semaphore("s_gx"))
        s_gx2 = es.enter_context(nc.semaphore("s_gx2"))
        s_t2 = es.enter_context(nc.semaphore("s_t2"))
        s_gp = es.enter_context(nc.semaphore("s_gp"))
        s_tr = es.enter_context(nc.semaphore("s_tr"))
        s_pe = es.enter_context(nc.semaphore("s_pe"))
        s_dv = es.enter_context(nc.semaphore("s_dv"))
        s_ac = es.enter_context(nc.semaphore("s_ac"))
        s_st = es.enter_context(nc.semaphore("s_st"))

        class Ctr:
            def __init__(self):
                self.v = 0

            def inc(self, k=1):
                self.v += k
                return self.v

        (c_in, c_i2, c_gx, c_gx2, c_t2, c_gp, c_tr, c_pe, c_dv, c_ac,
         c_st) = (Ctr() for _ in range(11))

        def dma_barrier(eng):
            for sem_, ctr_ in ((s_in, c_in), (s_i2, c_i2), (s_gx, c_gx),
                               (s_gx2, c_gx2), (s_tr, c_tr), (s_t2, c_t2),
                               (s_gp, c_gp), (s_st, c_st)):
                if ctr_.v:
                    eng.wait_ge(sem_, ctr_.v)

        blk = es.enter_context(nc.Block())

        esA = contextlib.ExitStack()

        def sbA(name, shape, dt=F32):
            return esA.enter_context(nc.sbuf_tensor(name, shape, dt))

        with esA:
            ktdup = sbA("ktdup", [128, NB], BF16)
            vaug = sbA("vaug", [128, 65 * NKB], BF16)
            vstg = sbA("vstg", [128, 2 * 64], BF16)
            wqs = sbA("wqs", [128, NDT * 512], BF16)
            wkvs = sbA("wkvs", [128, NDT * 128], BF16)
            wos = sbA("wos", [128, 4 * D], BF16)
            nwb = sbA("nwb_s", [128, D])
            nbb = sbA("nbb_s", [128, D])
            fat = sbA("fat", [16, 6 * 512])
            zrow = sbA("zrow", [65, 4 * 512])
            att = sbA("att", [128, D])
            attjunk = sbA("attjunk", [128, D], BF16)
            xot = sbA("xot", [128, 2 * D])

            esA1 = contextlib.ExitStack()

            def sbA1(name, shape, dt=F32):
                return esA1.enter_context(nc.sbuf_tensor(name, shape, dt))

            with esA1:
                wraw = sbA1("wraw", [128, NDT * 512])
                kvraw = sbA1("kvraw", [128, NB], BF16)
                xsf = sbA1("xsf", [128, 2 * 512])
                xb = sbA1("xb", [128, 4 * 512], BF16)
                x2b = sbA1("x2b", [128, 4 * 512], BF16)
                bcs = sbA1("bcs", [128, 2 * 512])
                bcrow = sbA1("bcrow", [1, 1024], BF16)
                pb12 = esA1.enter_context(nc.psum_tensor("pb12", [128, 1024], F32))

                @blk.gpsimd
                def _(g):
                    for dst, src in ((cwc, cw_h), (cbc, cb_h), (nwc, nw_h),
                                     (nbc, nb_h), (nwb, nwb_h), (nbb, nbb_h),
                                     (hmask, hm_h)):
                        g.dma_start(out=dst[:, :], in_=src[:, :]).then_inc(s_gp, 16)
                        c_gp.inc(16)
                    g.memset(ones_r[:, :], 1.0)
                    g.memset(ones_c[:, :], 1.0)
                    g.memset(halos[:, :], 0.0)
                    g.memset(colb[:, :], 0.0)
                    g.memset(colpad[:, :], 0.0)
                    g.dma_start(out=wraw[:, :], in_=wq_h[:, :]).then_inc(s_gp, 16)
                    c_gp.inc(16)
                P0_GP = c_gp.v

                @blk.vector
                def _(v):
                    v.wait_ge(s_gp, P0_GP)
                    v.tensor_copy(out=wqs[:, :], in_=wraw[:, :]).then_inc(s_dv, 1)
                P0_WQ = c_dv.inc()

                @blk.gpsimd
                def _(g):
                    g.wait_ge(s_dv, P0_WQ)
                    g.dma_start(out=wraw[:, 0:NDT * 128], in_=wkv_h[:, :]).then_inc(s_gp, 16)
                    c_gp.inc(16)
                P0_GP2 = c_gp.v

                @blk.vector
                def _(v):
                    v.wait_ge(s_gp, P0_GP2)
                    v.tensor_copy(out=wkvs[:, :], in_=wraw[:, 0:NDT * 128])
                    v.tensor_copy(out=colbb[:, 0:NDT], in_=nbc[:, :]).then_inc(s_dv, 1)
                P0_KV = c_dv.inc()

                @blk.tensor
                def _(t):
                    t.wait_ge(s_dv, P0_KV)
                    for dt in range(NDT):
                        inst = t.matmul(pb12[0:1, 0:128], colbb[:, dt:dt + 1],
                                        wkvs[:, dt * 128:(dt + 1) * 128],
                                        start=(dt == 0), stop=(dt == NDT - 1))
                    inst.then_inc(s_pe, 1)
                P0_VB = c_pe.inc()

                @blk.vector
                def _(v):
                    v.wait_ge(s_pe, P0_VB)
                    for dt in range(NDT):
                        v.tensor_scalar(out=wkvs[:, dt * 128:(dt + 1) * 128],
                                        in0=wkvs[:, dt * 128:(dt + 1) * 128],
                                        scalar1=nwc[:, dt:dt + 1], scalar2=None,
                                        op0=OP.mult)
                    v.tensor_copy(out=wkvs[:, 0:1], in_=wkvs[:, 0:1]).then_inc(s_dv, 1)
                P0_KVS = c_dv.inc()

                @blk.gpsimd
                def _(g):
                    g.wait_ge(s_dv, P0_KVS)
                    g.dma_start(out=wraw[:, 0:4 * D], in_=wo_h[:, :]).then_inc(s_gp, 16)
                    c_gp.inc(16)
                P0_GP3 = c_gp.v

                @blk.vector
                def _(v):
                    v.wait_ge(s_gp, P0_GP3)
                    v.tensor_copy(out=wos[:, :], in_=wraw[:, 0:4 * D]).then_inc(s_dv, 1)
                P0_DVE = c_dv.inc()

                @blk.tensor
                def _(t):
                    t.wait_ge(s_dv, P0_DVE)
                    for dt in range(NDT):
                        inst = t.matmul(pb12[0:1, 512:640], ones_c[:, :],
                                        wkvs[:, dt * 128:(dt + 1) * 128],
                                        start=(dt == 0), stop=(dt == NDT - 1))
                    inst.then_inc(s_pe, 1)
                P0_UVB = c_pe.inc()

                @blk.vector
                def _(v):
                    v.wait_ge(s_pe, P0_UVB)
                    v.tensor_copy(out=zrow[0:1, 0:128], in_=pb12[0:1, 0:128])
                    v.tensor_copy(out=zrow[0:1, 128:256],
                                  in_=pb12[0:1, 512:640]).then_inc(s_dv, 1)
                P0_UVROW = c_dv.inc()

                @blk.gpsimd
                def _(g):
                    g.wait_ge(s_dv, P0_UVROW)
                    g.dma_start(out=scr_h[0:256], in_=zrow[0:1, 0:256]).then_inc(s_gp, 16)
                    c_gp.inc(16)
                    g.wait_ge(s_gp, c_gp.v)
                    g.dma_start(out=colb[:, CB_VB:CB_VB + 1],
                                in_=scr_h[0:128].rearrange("(p o) -> p o", o=1)
                                ).then_inc(s_gp, 16)
                    c_gp.inc(16)
                    g.dma_start(out=colb[:, CB_U:CB_U + 1],
                                in_=scr_h[128:256].rearrange("(p o) -> p o", o=1)
                                ).then_inc(s_gp, 16)
                    c_gp.inc(16)
                P0_UVCOL = c_gp.v

                @blk.vector
                def _(v):
                    v.wait_ge(s_gp, P0_UVCOL)
                    v.tensor_scalar(out=colb[:, CB_NEGU:CB_NEGU + 1],
                                    in0=colb[:, CB_U:CB_U + 1], scalar1=-1.0,
                                    scalar2=None, op0=OP.mult).then_inc(s_dv, 1)
                P0_NEGU = c_dv.inc()

                P1_LOAD, P1_CAST, P1_MM, P1_EVAC, P1_ROW = {}, {}, {}, {}, {}
                for ch in range(NSC):
                    for dt in range(NDT):
                        li = ch * NDT + dt
                        sx = (li % 2) * 512
                        bx = (li % 4) * 512

                        @blk.sync
                        def _(sy, ch=ch, dt=dt, li=li, sx=sx):
                            if li >= 2:
                                sy.wait_ge(s_dv, P1_CAST[li - 2])
                            sem, ctr = (s_in, c_in) if li % 2 == 0 else (s_i2, c_i2)
                            sy.dma_start(
                                out=xsf[:, sx:sx + 512],
                                in_=xT_h[dt * 128:(dt + 1) * 128,
                                         ch * 512:(ch + 1) * 512]).then_inc(sem, 16)
                            ctr.inc(16)
                            P1_LOAD[li] = ctr.v

                        @blk.vector
                        def _(v, ch=ch, dt=dt, li=li, sx=sx, bx=bx):
                            v.wait_ge(s_in if li % 2 == 0 else s_i2, P1_LOAD[li])
                            if li == 0:
                                v.wait_ge(s_dv, P0_NEGU)
                            if li >= 4:
                                v.wait_ge(s_pe, P1_MM[li - 4])
                            v.tensor_copy(out=xb[:, bx:bx + 512], in_=xsf[:, sx:sx + 512])
                            v.tensor_tensor(out=x2b[:, bx:bx + 512],
                                            in0=xb[:, bx:bx + 512],
                                            in1=xb[:, bx:bx + 512],
                                            op=OP.mult).then_inc(s_dv, 1)
                            c_dv.inc()
                            P1_CAST[li] = c_dv.v

                        @blk.tensor
                        def _(t, ch=ch, dt=dt, li=li, bx=bx):
                            t.wait_ge(s_dv, P1_CAST[li])
                            if dt == 0 and ch >= 1:
                                t.wait_ge(s_dv, P1_EVAC[ch - 1])
                            st_, sp = (dt == 0), (dt == NDT - 1)
                            t.matmul(pb12[0:1, 0:512], ones_c[:, :], xb[:, bx:bx + 512],
                                     start=st_, stop=sp)
                            t.matmul(pb12[0:1, 512:1024], ones_c[:, :],
                                     x2b[:, bx:bx + 512], start=st_, stop=sp)
                            t.matmul(pacc[:, 0:512], wkvs[:, dt * 128:(dt + 1) * 128],
                                     xb[:, bx:bx + 512], start=st_,
                                     stop=sp).then_inc(s_pe, 1)
                            P1_MM[li] = c_pe.inc()

                    @blk.vector
                    def _(v, ch=ch):
                        v.wait_ge(s_pe, P1_MM[ch * NDT + NDT - 1])
                        if ch >= 1:
                            v.wait_ge(s_gp, P1_ROW[ch - 1])
                        v.tensor_copy(out=zrow[0:1, 0:512], in_=pb12[0:1, 0:512])
                        v.tensor_copy(out=zrow[0:1, 512:1024], in_=pb12[0:1, 512:1024])
                        v.tensor_copy(out=kvraw[:, ch * 512:(ch + 1) * 512],
                                      in_=pacc[:, 0:512]).then_inc(s_dv, 1)
                        c_dv.inc()
                        P1_EVAC[ch] = c_dv.v

                    @blk.gpsimd
                    def _(g, ch=ch):
                        g.wait_ge(s_dv, P1_EVAC[ch])
                        g.dma_start(out=scr_h[1024 + ch * 512:1024 + (ch + 1) * 512],
                                    in_=zrow[0:1, 0:512]).then_inc(s_gp, 16)
                        c_gp.inc(16)
                        g.dma_start(out=scr_h[1024 + NB + ch * 512:
                                              1024 + NB + (ch + 1) * 512],
                                    in_=zrow[0:1, 512:1024]).then_inc(s_gp, 16)
                        c_gp.inc(16)
                        P1_ROW[ch] = c_gp.v

                @blk.gpsimd
                def _(g):
                    g.wait_ge(s_gp, P1_ROW[NSC - 1])
                    g.dma_start(out=fat[0:NSC, 0:512],
                                in_=scr_h[1024:1024 + NB].rearrange("(a b) -> a b", a=NSC)
                                ).then_inc(s_gp, 16)
                    c_gp.inc(16)
                    g.dma_start(out=fat[0:NSC, 512:1024],
                                in_=scr_h[1024 + NB:1024 + 2 * NB].rearrange(
                                    "(a b) -> a b", a=NSC)).then_inc(s_gp, 16)
                    c_gp.inc(16)
                P1_FAT = c_gp.v

                SXr = fat[0:NSC, 0:512]
                VARr = fat[0:NSC, 512:1024]
                Ar = fat[0:NSC, 1024:1536]
                Cr = fat[0:NSC, 1536:2048]
                Mr = fat[0:NSC, 2048:2560]
                A1r = fat[0:NSC, 2560:3072]

                @blk.vector
                def _(v):
                    v.wait_ge(s_gp, P1_FAT)
                    v.tensor_scalar(out=Mr, in0=SXr, scalar1=1.0 / D, scalar2=None,
                                    op0=OP.mult)
                    v.scalar_tensor_tensor(out=Ar, in0=Mr, scalar=1.0, op0=OP.mult,
                                           in1=Mr, op1=OP.mult)
                    v.tensor_scalar(out=VARr, in0=VARr, scalar1=1.0 / D, scalar2=None,
                                    op0=OP.mult)
                    v.tensor_tensor(out=VARr, in0=VARr, in1=Ar, op=OP.subtract)
                    v.tensor_scalar(out=Cr, in0=VARr, scalar1=EPS, scalar2=None,
                                    op0=OP.add)
                    v.reciprocal(out=A1r, in_=Cr).then_inc(s_dv, 1)
                P1_R = c_dv.inc()

                @blk.scalar
                def _(s):
                    s.wait_ge(s_dv, P1_R)
                    s.activation(out=A1r, in_=A1r, func=AF.Sqrt, bias=0.0,
                                 scale=1.0).then_inc(s_ac, 1)
                P1_A1 = c_ac.inc()

                @blk.vector
                def _(v):
                    v.wait_ge(s_ac, P1_A1)
                    v.scalar_tensor_tensor(out=Cr, in0=A1r, scalar=1.0, op0=OP.mult,
                                           in1=A1r, op1=OP.mult)
                    v.tensor_tensor(out=Cr, in0=VARr, in1=Cr, op=OP.mult)
                    v.tensor_scalar(out=Cr, in0=Cr, scalar1=EPS, scalar2=None,
                                    op0=OP.add)
                    v.reciprocal(out=Ar, in_=Cr).then_inc(s_dv, 1)
                P1_R2 = c_dv.inc()

                @blk.scalar
                def _(s):
                    s.wait_ge(s_dv, P1_R2)
                    s.activation(out=Ar, in_=Ar, func=AF.Sqrt, bias=0.0,
                                 scale=1.0).then_inc(s_ac, 1)
                P1_A2 = c_ac.inc()

                @blk.vector
                def _(v):
                    v.wait_ge(s_ac, P1_A2)
                    v.tensor_tensor(out=Ar, in0=A1r, in1=Ar, op=OP.mult)
                    v.tensor_tensor(out=Cr, in0=Mr, in1=Ar, op=OP.mult)
                    v.tensor_copy(out=fatb[0:NSC, 0:512], in_=Ar)
                    v.tensor_copy(out=fatb[0:NSC, 512:1024], in_=Cr).then_inc(s_dv, 1)
                P1_AC = c_dv.inc()

                P1C_BC, P1C_KV, P1C_RW = {}, {}, {}
                for ch in range(NSC):
                    @blk.gpsimd
                    def _(g, ch=ch):
                        g.wait_ge(s_dv, P1_AC if ch == 0 else P1C_KV[ch - 1])
                        g.dma_start(out=bcrow[0:1, :],
                                    in_=fatb[ch:ch + 1, 0:1024]).then_inc(s_gp, 16)
                        c_gp.inc(16)
                        P1C_RW[ch] = c_gp.v

                    @blk.tensor
                    def _(t, ch=ch):
                        t.wait_ge(s_gp, P1C_RW[ch])
                        t.matmul(pb12[:, 0:512], ones_r[0:1, :],
                                 bcrow[0:1, 0:512], start=True, stop=True)
                        t.matmul(pb12[:, 512:1024], ones_r[0:1, :],
                                 bcrow[0:1, 512:1024], start=True,
                                 stop=True).then_inc(s_pe, 1)
                        P1C_BC[ch] = c_pe.inc()

                    @blk.vector
                    def _(v, ch=ch):
                        v.wait_ge(s_pe, P1C_BC[ch])
                        cs_ = slice(ch * 512, (ch + 1) * 512)
                        v.tensor_tensor(out=bcs[:, 0:512], in0=kvraw[:, cs_],
                                        in1=pb12[:, 0:512], op=OP.mult)
                        v.scalar_tensor_tensor(out=bcs[:, 512:1024],
                                               in0=pb12[:, 512:1024],
                                               scalar=colb[:, CB_NEGU:CB_NEGU + 1],
                                               op0=OP.mult, in1=bcs[:, 0:512],
                                               op1=OP.add)
                        v.tensor_scalar(out=ktdup[0:64, cs_], in0=bcs[0:64, 512:1024],
                                        scalar1=colb[0:64, CB_VB:CB_VB + 1],
                                        scalar2=None, op0=OP.add)
                        v.tensor_scalar(out=ktdup[64:128, cs_],
                                        in0=bcs[64:128, 512:1024],
                                        scalar1=colb[64:128, CB_VB:CB_VB + 1],
                                        scalar2=None, op0=OP.add).then_inc(s_dv, 1)
                        P1C_KV[ch] = c_dv.inc()
                P1_DONE = c_dv.v

                P1_VTT, P1_VTC = {}, {}
                for j in range(NKB):
                    @blk.sync
                    def _(sy, j=j):
                        if j == 0:
                            sy.wait_ge(s_dv, P1_DONE)
                        if j >= 2:
                            sy.wait_ge(s_dv, P1_VTC[j - 2])
                        sem, ctr = (s_in, c_in) if j % 2 == 0 else (s_i2, c_i2)
                        sy.dma_start_transpose(out=vstg[:, (j % 2) * 64:(j % 2) * 64 + 64],
                                               in_=ktdup[64:128, j * 128:(j + 1) * 128]
                                               ).then_inc(sem, 16)
                        ctr.inc(16)
                        P1_VTT[j] = ctr.v

                    @blk.vector
                    def _(v, j=j):
                        v.wait_ge(s_in if j % 2 == 0 else s_i2, P1_VTT[j])
                        v.tensor_copy(out=vaug[:, j * 65:j * 65 + 64],
                                      in_=vstg[:, (j % 2) * 64:(j % 2) * 64 + 64]
                                      ).then_inc(s_dv, 1)
                        c_dv.inc()
                        P1_VTC[j] = c_dv.v
                P1_VT_DV = c_dv.v

                @blk.vector
                def _(v):
                    v.wait_ge(s_dv, P1_VT_DV)
                    v.tensor_copy(out=ktdup[64:128, :],
                                  in_=ktdup[0:64, :]).then_inc(s_dv, 1)
                P1_KDUP = c_dv.inc()

                @blk.gpsimd
                def _(g):
                    g.wait_ge(s_dv, P1_KDUP)
                    for j in range(NKB):
                        g.memset(vaug[:, j * 65 + 64:j * 65 + 65], 1.0)
                    g.memset(scst[:, :], 0.0)

                @blk.gpsimd
                def _(g):
                    g.dma_start(out=scst[0:1, 0:1],
                                in_=scr_h[0:1].rearrange("(p o) -> p o", o=1)
                                ).then_inc(s_gp, 16)
                    c_gp.inc(16)
                P1_VAUG = c_gp.v

            esA2 = contextlib.ExitStack()

            def sbA2(name, shape, dt=F32):
                return esA2.enter_context(nc.sbuf_tensor(name, shape, dt))

            with esA2:
                qtd = sbA2("qtd", [128, 8 * T], BF16)
                xn2b = sbA2("xn2b", [128, NDT * 512], BF16)
                expb = sbA2("expb", [128, 2 * 2048], BF16)
                rowr = sbA2("rowr", [1, 2048], BF16)
                rz8 = sbA2("rz8", [8, 512])
                rzr = sbA2("rzr", [8, 512])
                rz8b = sbA2("rz8b", [8, 512], BF16)
                rzb = rowr
                rzbc = sbA2("rzbc", [64, 512])
                xqn = sbA2("xqn", [128, 2 * D], BF16)
                xstg = sbA2("xstg", [128, 2 * D], BF16)
                xn2n = sbA2("xn2n", [128, 2 * D], BF16)
                xtmp = sbA2("xtmp", [128, D])
                woin = sbA2("woin", [128, 4 * 512], BF16)
                oddst = sbA2("oddst", [64, 512], BF16)

                esP2 = contextlib.ExitStack()
                pbt = esP2.enter_context(nc.psum_tensor("pbt", [128, 1024], F32))
                zi32 = esP2.enter_context(nc.psum_tensor("zi32p", [128, 1024], I32))
                with esP2:
                    P2_LD, P2_ST, P2_SQ1, P2_CM, P2_SQ2 = {}, {}, {}, {}, {}
                    P2_QX, P2_TRX, P2_TRN, P2_QT, P2_QTD, P2_QTDG = {}, {}, {}, {}, {}, {}
                    P2_TRXC = {}

                    for tt in range(NTT):
                        par = tt % 2
                        t16 = tt % 16

                        @blk.gpsimd
                        def _(g, tt=tt, par=par):
                            if tt == 0:
                                g.wait_ge(s_gp, P1_VAUG)
                                dma_barrier(g)
                            if tt >= 2:
                                g.wait_ge(s_dv, P2_QX[tt - 2])
                            sem, ctr = (s_gx, c_gx) if tt % 2 == 0 else (s_gx2, c_gx2)
                            g.dma_start(out=xot[:, par * D:(par + 1) * D],
                                        in_=xow_h[tt * 128:(tt + 1) * 128, :]
                                        ).then_inc(sem, 16)
                            ctr.inc(16)
                            P2_LD[tt] = ctr.v

                        @blk.vector
                        def _(v, tt=tt, par=par, t16=t16):
                            v.wait_ge(s_gx if tt % 2 == 0 else s_gx2, P2_LD[tt])
                            xo = xot[:, par * D:(par + 1) * D]
                            m1 = colb[:, CB_M1 + t16:CB_M1 + t16 + 1]
                            a1 = colb[:, CB_A1 + t16:CB_A1 + t16 + 1]
                            t0 = colb[:, CB_T0 + t16:CB_T0 + t16 + 1]
                            t1c = colb[:, CB_T1 + t16:CB_T1 + t16 + 1]
                            mx = colb[:, CB_MX + t16:CB_MX + t16 + 1]
                            g_ = colb[:, CB_G + t16:CB_G + t16 + 1]
                            v.reduce_sum(out=t0, in_=xo, axis=AX.X)
                            v.tensor_tensor(out=xtmp[:, :], in0=xo, in1=xo,
                                            op=OP.mult)
                            v.reduce_sum(out=t1c, in_=xtmp[:, :], axis=AX.X)
                            v.reduce_max(out=mx, in_=xo, axis=AX.X)
                            v.tensor_reduce(out=g_, in_=xo, axis=AX.X, op=OP.min)
                            v.tensor_scalar(out=m1, in0=t0, scalar1=1.0 / D,
                                            scalar2=None, op0=OP.mult)
                            v.tensor_scalar(out=t1c, in0=t1c, scalar1=1.0 / D,
                                            scalar2=None, op0=OP.mult)
                            v.scalar_tensor_tensor(out=a1, in0=m1, scalar=1.0,
                                                   op0=OP.mult, in1=m1, op1=OP.mult)
                            v.tensor_tensor(out=mx, in0=mx, in1=m1, op=OP.subtract)
                            v.tensor_tensor(out=g_, in0=m1, in1=g_, op=OP.subtract)
                            v.tensor_tensor(out=a1, in0=t1c, in1=a1, op=OP.subtract)
                            v.tensor_tensor(out=mx, in0=mx, in1=g_, op=OP.max)
                            v.tensor_scalar(out=a1, in0=a1, scalar1=EPS, scalar2=None,
                                            op0=OP.add)
                            v.tensor_copy(out=t0, in_=t1c)
                            v.reciprocal(out=a1, in_=a1).then_inc(s_dv, 1)
                            c_dv.inc()
                            P2_ST[tt] = c_dv.v

                        @blk.scalar
                        def _(s, tt=tt, t16=t16):
                            s.wait_ge(s_dv, P2_ST[tt])
                            a1 = colb[:, CB_A1 + t16:CB_A1 + t16 + 1]
                            s.activation(out=a1, in_=a1, func=AF.Sqrt, bias=0.0,
                                         scale=1.0).then_inc(s_ac, 1)
                            c_ac.inc()
                            P2_SQ1[tt] = c_ac.v

                        @blk.vector
                        def _(v, tt=tt, t16=t16):
                            v.wait_ge(s_ac, P2_SQ1[tt])
                            a1 = colb[:, CB_A1 + t16:CB_A1 + t16 + 1]
                            t0 = colb[:, CB_T0 + t16:CB_T0 + t16 + 1]
                            t1c = colb[:, CB_T1 + t16:CB_T1 + t16 + 1]
                            g_ = colb[:, CB_G + t16:CB_G + t16 + 1]
                            mx = colb[:, CB_MX + t16:CB_MX + t16 + 1]
                            iq = colb[:, CB_IQ + t16:CB_IQ + t16 + 1]
                            v.scalar_tensor_tensor(out=t0, in0=a1, scalar=1.0,
                                                   op0=OP.mult, in1=a1, op1=OP.mult)
                            v.tensor_tensor(out=mx, in0=mx, in1=a1, op=OP.mult)
                            v.reciprocal(out=t1c, in_=t0)
                            v.tensor_scalar(out=mx, in0=mx, scalar1=1e-5,
                                            scalar2=None, op0=OP.max)
                            v.tensor_scalar(out=t1c, in0=t1c, scalar1=-EPS,
                                            scalar2=None, op0=OP.add)
                            v.tensor_scalar(out=iq, in0=mx, scalar1=1.0 / 127.0,
                                            scalar2=None, op0=OP.mult)
                            v.tensor_tensor(out=t0, in0=t1c, in1=t0, op=OP.mult)
                            v.reciprocal(out=g_, in_=mx)
                            v.tensor_scalar(out=t0, in0=t0, scalar1=EPS, scalar2=None,
                                            op0=OP.add)
                            v.tensor_scalar(out=g_, in0=g_, scalar1=127.0,
                                            scalar2=None, op0=OP.mult)
                            v.reciprocal(out=t0, in_=t0)
                            v.tensor_tensor(out=g_, in0=a1, in1=g_,
                                            op=OP.mult).then_inc(s_dv, 1)
                            c_dv.inc()
                            P2_CM[tt] = c_dv.v

                        @blk.scalar
                        def _(s, tt=tt, t16=t16):
                            s.wait_ge(s_dv, P2_CM[tt])
                            t0 = colb[:, CB_T0 + t16:CB_T0 + t16 + 1]
                            s.activation(out=t0, in_=t0, func=AF.Sqrt, bias=0.0,
                                         scale=1.0).then_inc(s_ac, 1)
                            c_ac.inc()
                            P2_SQ2[tt] = c_ac.v

                        @blk.vector
                        def _(v, tt=tt, par=par, t16=t16):
                            v.wait_ge(s_ac, P2_SQ2[tt])
                            if tt >= 2:
                                v.wait_ge(s_tr if tt % 2 == 0 else s_t2,
                                          P2_TRN[tt - 2])
                                v.wait_ge(s_in if tt % 2 == 0 else s_i2,
                                          P2_TRX[tt - 2])
                            xo = xot[:, par * D:(par + 1) * D]
                            m1 = colb[:, CB_M1 + t16:CB_M1 + t16 + 1]
                            a1 = colb[:, CB_A1 + t16:CB_A1 + t16 + 1]
                            aa = colb[:, CB_A + t16:CB_A + t16 + 1]
                            g_ = colb[:, CB_G + t16:CB_G + t16 + 1]
                            t0 = colb[:, CB_T0 + t16:CB_T0 + t16 + 1]
                            v.tensor_tensor(out=aa, in0=a1, in1=t0, op=OP.mult)
                            v.tensor_scalar(out=xtmp[:, :], in0=xo, scalar1=m1,
                                            scalar2=g_, op0=OP.subtract, op1=OP.mult)
                            v.tensor_scalar(out=xtmp[:, :], in0=xtmp[:, :],
                                            scalar1=128.5, scalar2=None, op0=OP.add)
                            v.tensor_copy(out=zi32[:, :], in_=xtmp[:, :])
                            v.tensor_copy(out=woin[:, 0:D], in_=zi32[:, :])
                            v.tensor_tensor(out=attjunk[:, :], in0=woin[:, 0:D],
                                            in1=xtmp[:, :], op=OP.is_le)
                            v.tensor_tensor(out=woin[:, 0:D], in0=woin[:, 0:D],
                                            in1=attjunk[:, :], op=OP.add)
                            v.tensor_scalar(out=xqn[:, par * D:(par + 1) * D],
                                            in0=woin[:, 0:D], scalar1=-129.0,
                                            scalar2=127.0, op0=OP.add, op1=OP.min)
                            v.tensor_scalar(out=xtmp[:, :], in0=xo, scalar1=m1,
                                            scalar2=aa, op0=OP.subtract, op1=OP.mult)
                            v.tensor_tensor(out=xtmp[:, :], in0=xtmp[:, :],
                                            in1=nwb[:, :], op=OP.mult)
                            v.tensor_tensor(out=xn2n[:, par * D:(par + 1) * D],
                                            in0=xtmp[:, :], in1=nbb[:, :],
                                            op=OP.add).then_inc(s_dv, 1)
                            c_dv.inc()
                            P2_QX[tt] = c_dv.v

                        @blk.scalar
                        def _(s, tt=tt, par=par):
                            s.wait_ge(s_dv, P2_QX[tt])
                            if tt >= 2:
                                s.wait_ge(s_dv, P2_TRXC[tt - 2])
                            if tt // 4 >= 1 and tt % 4 == 0:
                                s.wait_ge(s_pe, P2_QT[tt // 4 - 1])
                            semx, ctrx = (s_in, c_in) if tt % 2 == 0 else \
                                (s_i2, c_i2)
                            for dt in range(NDT):
                                s.dma_start_transpose(
                                    out=xstg[:, par * D + dt * 128:
                                             par * D + (dt + 1) * 128],
                                    in_=xqn[:, par * D + dt * 128:
                                            par * D + (dt + 1) * 128]
                                    ).then_inc(semx, 16)
                                ctrx.inc(16)
                            P2_TRX[tt] = ctrx.v
                            sem, ctr = (s_tr, c_tr) if tt % 2 == 0 else (s_t2, c_t2)
                            for dt in range(NDT):
                                s.dma_start_transpose(
                                    out=xn2b[:, dt * 512 + (tt % 4) * 128:
                                             dt * 512 + (tt % 4 + 1) * 128],
                                    in_=xn2n[:, par * D + dt * 128:
                                             par * D + (dt + 1) * 128]
                                    ).then_inc(sem, 16)
                                ctr.inc(16)
                            P2_TRN[tt] = ctr.v

                        @blk.vector
                        def _(v, tt=tt, par=par):
                            v.wait_ge(s_in if tt % 2 == 0 else s_i2, P2_TRX[tt])
                            v.tensor_copy(
                                out=xnq[:, :].rearrange("p (n t) -> p n t", n=NDT)[
                                    :, :, 1 + tt * 128:1 + (tt + 1) * 128],
                                in_=xstg[:, par * D:(par + 1) * D].rearrange(
                                    "p (n t) -> p n t", n=NDT)).then_inc(s_dv, 1)
                            c_dv.inc()
                            P2_TRXC[tt] = c_dv.v

                        if tt % 4 == 3:
                            qc = tt // 4
                            for qt in range(4):
                                hev, hod = 2 * qt, 2 * qt + 1

                                @blk.tensor
                                def _(t, qc=qc, qt=qt, tt=tt):
                                    if qt == 0:
                                        t.wait_ge(s_tr, P2_TRN[tt - 1])
                                        t.wait_ge(s_t2, P2_TRN[tt])
                                    if qt >= 1:
                                        t.wait_ge(s_dv, P2_QTD[(qc, qt - 1)])
                                    elif qc >= 1:
                                        t.wait_ge(s_dv, P2_QTD[(qc - 1, 3)])
                                    for dt in range(NDT):
                                        inst = t.matmul(
                                            pacc[:, 512:1024],
                                            wqs[:, dt * 512 + qt * 128:
                                                dt * 512 + (qt + 1) * 128],
                                            xn2b[:, dt * 512:(dt + 1) * 512],
                                            start=(dt == 0), stop=(dt == NDT - 1))
                                    inst.then_inc(s_pe, 1)
                                    c_pe.inc()
                                    P2_QT[(qc, qt)] = c_pe.v
                                    if qt == 3:
                                        P2_QT[qc] = c_pe.v

                                @blk.vector
                                def _(v, qc=qc, hev=hev, hod=hod, qt=qt):
                                    v.wait_ge(s_pe, P2_QT[(qc, qt)])
                                    v.tensor_copy(
                                        out=qtd[0:64, hev * T + qc * 512:
                                                hev * T + qc * 512 + 512],
                                        in_=pacc[0:64, 512:1024])
                                    v.tensor_copy(
                                        out=qtd[64:128, hod * T + qc * 512:
                                                hod * T + qc * 512 + 512],
                                        in_=pacc[64:128, 512:1024]).then_inc(s_dv, 1)
                                    c_dv.inc()
                                    P2_QTD[(qc, qt)] = c_dv.v

                                @blk.gpsimd
                                def _(g, qc=qc, hev=hev, hod=hod, qt=qt):
                                    g.wait_ge(s_dv, P2_QTD[(qc, qt)])
                                    g.dma_start(
                                        out=qtd[64:128, hev * T + qc * 512:
                                                hev * T + qc * 512 + 512],
                                        in_=qtd[0:64, hev * T + qc * 512:
                                                hev * T + qc * 512 + 512]
                                        ).then_inc(s_gp, 16)
                                    c_gp.inc(16)
                                    g.dma_start(
                                        out=qtd[0:64, hod * T + qc * 512:
                                                hod * T + qc * 512 + 512],
                                        in_=qtd[64:128, hod * T + qc * 512:
                                                hod * T + qc * 512 + 512]
                                        ).then_inc(s_gp, 16)
                                    c_gp.inc(16)
                                    P2_QTDG[(qc, qt)] = c_gp.v
                    P2_TILE_DONE = c_dv.v
                    P2_QTD_GP = c_gp.v

                    @blk.vector
                    def _(v):
                        v.wait_ge(s_dv, P2_TILE_DONE)
                        v.tensor_copy(out=colpad[:, 0:16],
                                      in_=colb[:, CB_IQ:CB_IQ + 16]).then_inc(s_dv, 1)
                    P2_IQB = c_dv.inc()

                    @blk.scalar
                    def _(s):
                        s.wait_ge(s_dv, P2_IQB)
                        s.dma_start_transpose(out=rowt[:, :],
                                              in_=colpad[:, :]).then_inc(s_tr, 16)
                        c_tr.inc(16)
                    P2_IQT = c_tr.v

                    @blk.gpsimd
                    def _(g):
                        g.wait_ge(s_tr, P2_IQT)
                        g.dma_start(out=rowr[0:1, 0:NTT * 128],
                                    in_=rowt[0:NTT, 0:128]).then_inc(s_gp, 16)
                        c_gp.inc(16)
                    P2_IQR = c_gp.v

                    P2_IQM, P2_IQE = {}, {}
                    for qc in range(NQC):
                        @blk.tensor
                        def _(t, qc=qc):
                            if qc == 0:
                                t.wait_ge(s_gp, P2_IQR)
                            else:
                                t.wait_ge(s_dv, P2_IQE[qc - 1])
                            t.matmul(pbt[:, 0:512], ones_r[0:1, :],
                                     rowr[0:1, qc * 512:(qc + 1) * 512], start=True,
                                     stop=True).then_inc(s_pe, 1)
                            P2_IQM[qc] = c_pe.inc()

                        @blk.vector
                        def _(v, qc=qc):
                            v.wait_ge(s_pe, P2_IQM[qc])
                            v.tensor_copy(out=invqsb[:, 1 + qc * 512:1 + (qc + 1) * 512],
                                          in_=pbt[:, 0:512]).then_inc(s_dv, 1)
                            c_dv.inc()
                            P2_IQE[qc] = c_dv.v
                    P2_IQDONE = c_dv.v

                    HL = {}
                    for sd in range(2):
                        frow, fcol = (NSC - 1, 511) if sd == 0 else (T // 512, 0)
                        xcol = NB - 1 if sd == 0 else T

                        @blk.gpsimd
                        def _(g, sd=sd, frow=frow, fcol=fcol, xcol=xcol):
                            g.wait_ge(s_dv, P2_IQDONE)
                            g.wait_ge(s_gp, c_gp.v)
                            if sd == 1:
                                g.wait_ge(s_dv, HL[(0, "rw")])
                            g.dma_start(out=scst[0:1, 8 * sd:8 * sd + 1],
                                        in_=fat[frow:frow + 1,
                                                2048 + fcol:2048 + fcol + 1]
                                        ).then_inc(s_gp, 16)
                            c_gp.inc(16)
                            g.dma_start(out=scst[0:1, 8 * sd + 1:8 * sd + 2],
                                        in_=fat[frow:frow + 1,
                                                2560 + fcol:2560 + fcol + 1]
                                        ).then_inc(s_gp, 16)
                            c_gp.inc(16)
                            for dt in range(NDT):
                                g.dma_start(out=halos[:, 16 * sd + dt:16 * sd + dt + 1],
                                            in_=xT_h[dt * 128:(dt + 1) * 128,
                                                     xcol:xcol + 1],
                                            allow_slow_non_contiguous=True
                                            ).then_inc(s_gp, 16)
                                c_gp.inc(16)
                            HL[(sd, "sc")] = c_gp.v

                        @blk.vector
                        def _(v, sd=sd):
                            v.wait_ge(s_gp, HL[(sd, "sc")])
                            v.tensor_copy(out=scstb[0:1, 4 * sd:4 * sd + 2],
                                          in_=scst[0:1, 8 * sd:8 * sd + 2]
                                          ).then_inc(s_dv, 1)
                            c_dv.inc()
                            HL[(sd, "cb")] = c_dv.v

                        @blk.tensor
                        def _(t, sd=sd):
                            t.wait_ge(s_dv, HL[(sd, "cb")])
                            t.matmul(pbt[:, 0:1], ones_r[0:1, :],
                                     scstb[0:1, 4 * sd:4 * sd + 1], start=True,
                                     stop=True)
                            t.matmul(pbt[:, 1:2], ones_r[0:1, :],
                                     scstb[0:1, 4 * sd + 1:4 * sd + 2], start=True,
                                     stop=True).then_inc(s_pe, 1)
                            HL[(sd, "rep")] = c_pe.inc()

                        @blk.vector
                        def _(v, sd=sd):
                            v.wait_ge(s_pe, HL[(sd, "rep")])
                            v.tensor_copy(out=halos[:, 32 + 4 * sd:32 + 4 * sd + 1],
                                          in_=pbt[:, 0:1])
                            v.tensor_copy(out=halos[:, 33 + 4 * sd:33 + 4 * sd + 1],
                                          in_=pbt[:, 1:2])
                            v.tensor_copy(out=colpad[:, 2:3],
                                          in_=colb[:, CB_M1:CB_M1 + 1])
                            v.tensor_scalar(out=halos[:, 16 * sd:16 * sd + NDT],
                                            in0=halos[:, 16 * sd:16 * sd + NDT],
                                            scalar1=halos[:, 32 + 4 * sd:
                                                          32 + 4 * sd + 1],
                                            scalar2=halos[:, 33 + 4 * sd:
                                                          33 + 4 * sd + 1],
                                            op0=OP.subtract, op1=OP.mult)
                            v.tensor_copy(out=colpad[:, 2:3],
                                          in_=colb[:, CB_M1:CB_M1 + 1])
                            v.reduce_max(out=halos[:, 34 + 4 * sd:34 + 4 * sd + 1],
                                         in_=halos[:, 16 * sd:16 * sd + NDT],
                                         axis=AX.X,
                                         apply_absolute_value=True).then_inc(s_dv, 1)
                            c_dv.inc()
                            HL[(sd, "xn")] = c_dv.v

                        @blk.gpsimd
                        def _(g, sd=sd):
                            g.wait_ge(s_dv, HL[(sd, "xn")])
                            g.dma_start(out=scr_h[512:640],
                                        in_=halos[:, 34 + 4 * sd:34 + 4 * sd + 1]
                                        ).then_inc(s_gp, 16)
                            c_gp.inc(16)
                            g.wait_ge(s_gp, c_gp.v)
                            g.dma_start(out=zrow[0:1, 1024:1152],
                                        in_=scr_h[512:640]).then_inc(s_gp, 16)
                            c_gp.inc(16)
                            HL[(sd, "bn")] = c_gp.v

                        @blk.vector
                        def _(v, sd=sd):
                            v.wait_ge(s_gp, HL[(sd, "bn")])
                            v.reduce_max(out=scst[0:1, 48 + sd:48 + sd + 1],
                                         in_=zrow[0:1, 1024:1152], axis=AX.X)
                            v.tensor_copy(out=colpad[:, 2:3],
                                          in_=colb[:, CB_M1:CB_M1 + 1])
                            v.tensor_scalar(out=scst[0:1, 48 + sd:48 + sd + 1],
                                            in0=scst[0:1, 48 + sd:48 + sd + 1],
                                            scalar1=1e-5, scalar2=None, op0=OP.max)
                            v.tensor_copy(out=colpad[:, 2:3],
                                          in_=colb[:, CB_M1:CB_M1 + 1])
                            v.reciprocal(out=scst[0:1, 50 + sd:50 + sd + 1],
                                         in_=scst[0:1, 48 + sd:48 + sd + 1])
                            v.tensor_scalar(out=scst[0:1, 52 + sd:52 + sd + 1],
                                            in0=scst[0:1, 48 + sd:48 + sd + 1],
                                            scalar1=1.0 / 127.0, scalar2=None,
                                            op0=OP.mult)
                            v.tensor_scalar(out=scst[0:1, 50 + sd:50 + sd + 1],
                                            in0=scst[0:1, 50 + sd:50 + sd + 1],
                                            scalar1=127.0, scalar2=None, op0=OP.mult)
                            v.tensor_copy(out=scstb[0:1, 9 + 2 * sd:9 + 2 * sd + 1],
                                          in_=scst[0:1, 52 + sd:52 + sd + 1])
                            v.tensor_copy(out=scstb[0:1, 8 + 2 * sd:8 + 2 * sd + 1],
                                          in_=scst[0:1, 50 + sd:50 + sd + 1]
                                          ).then_inc(s_dv, 1)
                            c_dv.inc()
                            HL[(sd, "rw")] = c_dv.v

                        @blk.tensor
                        def _(t, sd=sd):
                            t.wait_ge(s_dv, HL[(sd, "rw")])
                            t.matmul(pbt[:, 2:3], ones_r[0:1, :],
                                     scstb[0:1, 8 + 2 * sd:8 + 2 * sd + 1],
                                     start=True, stop=True)
                            t.matmul(pbt[:, 3:4], ones_r[0:1, :],
                                     scstb[0:1, 9 + 2 * sd:9 + 2 * sd + 1],
                                     start=True, stop=True).then_inc(s_pe, 1)
                            HL[(sd, "q")] = c_pe.inc()

                        @blk.vector
                        def _(v, sd=sd):
                            v.wait_ge(s_pe, HL[(sd, "q")])
                            xq_col = 0 if sd == 0 else T + 1
                            q_sl = halos[:, 16 * sd:16 * sd + NDT]
                            f_sl = halos[:, 32:32 + NDT]
                            t_sl = halos[:, 24:24 + NDT]

                            def spc():
                                v.tensor_copy(out=colpad[:, 2:3],
                                              in_=colb[:, CB_M1:CB_M1 + 1])
                            v.tensor_scalar(out=q_sl, in0=q_sl,
                                            scalar1=pbt[:, 2:3], scalar2=128.5,
                                            op0=OP.mult, op1=OP.add)
                            v.tensor_copy(out=invqsb[:, xq_col:xq_col + 1],
                                          in_=pbt[:, 3:4])
                            v.tensor_copy(out=zi32[:, 0:NDT], in_=q_sl)
                            spc()
                            v.tensor_copy(out=f_sl, in_=zi32[:, 0:NDT])
                            spc()
                            v.tensor_tensor(out=t_sl, in0=f_sl, in1=q_sl,
                                            op=OP.is_le)
                            spc()
                            v.tensor_tensor(out=f_sl, in0=f_sl, in1=t_sl,
                                            op=OP.add)
                            spc()
                            v.tensor_scalar(out=q_sl, in0=f_sl, scalar1=-129.0,
                                            scalar2=127.0, op0=OP.add, op1=OP.min)
                            spc()
                            v.tensor_scalar(out=q_sl, in0=q_sl,
                                            scalar1=hmask[:, sd:sd + 1],
                                            scalar2=None, op0=OP.mult)
                            spc()
                            v.tensor_copy(
                                out=xnq[:, :].rearrange("p (n t) -> p n t",
                                                        n=NDT)[:, :, xq_col],
                                in_=q_sl).then_inc(s_dv, 1)
                            c_dv.inc()
                            HL[(sd, "done")] = c_dv.v
                    P2_DONE = c_dv.v

                pb3 = esA2.enter_context(nc.psum_tensor("pb3", [128, 2048], F32))
                P3_SIM, P3_EXP, P3_AVD = {}, {}, {}
                P3_ZE, P3_RZ, P3_RZB, P3_PRZ, P3_WOIN, P3_ODD = {}, {}, {}, {}, {}, {}
                P3_WO, P3_EV1, P3_EV2, P3_SP = {}, {}, {}, {}

                sets = [(qc, st_) for qc in range(NQC) for st_ in range(2)]
                for si, (qc, st_) in enumerate(sets):
                    prev = sets[si - 1] if si > 0 else None
                    for kb in range(NKB):
                        @blk.tensor
                        def _(t, qc=qc, st_=st_, kb=kb, prev=prev, si=si):
                            if si == 0 and kb == 0:
                                t.wait_ge(s_dv, P2_DONE)
                                t.wait_ge(s_gp, P2_QTD_GP)
                            if kb == 0 and prev is not None:
                                t.wait_ge(s_dv, P3_WOIN[(prev[0], prev[1], 3)])
                                t.wait_ge(s_gp, P3_ODD[(prev[0], prev[1], 3)])
                                if st_ == 0:
                                    t.wait_ge(s_ac, P3_EV1[(qc - 1, 3)])
                            if kb > 0:
                                t.wait_ge(s_ac, P3_EXP[(qc, st_, kb - 1)])
                                for j in range(4):
                                    t.matmul(pacc[0:65, j * 512:j * 512 + 512],
                                             vaug[:, (kb - 1) * 65:(kb - 1) * 65 + 65],
                                             expb[:, ((kb - 1) % 2) * 2048 + j * 512:
                                                  ((kb - 1) % 2) * 2048 + (j + 1) * 512],
                                             start=(kb - 1 == 0),
                                             stop=(kb - 1 == NKB - 1))
                            for j in range(4):
                                h = 4 * st_ + j
                                inst = t.matmul(
                                    pb3[:, j * 512:(j + 1) * 512],
                                    ktdup[:, kb * 128:(kb + 1) * 128],
                                    qtd[:, h * T + qc * 512:h * T + (qc + 1) * 512],
                                    start=True, stop=True)
                            inst.then_inc(s_pe, 1)
                            P3_SIM[(qc, st_, kb)] = c_pe.inc()

                        @blk.scalar
                        def _(s, qc=qc, st_=st_, kb=kb):
                            s.wait_ge(s_pe, P3_SIM[(qc, st_, kb)])
                            s.activation(out=expb[:, (kb % 2) * 2048:
                                                  (kb % 2) * 2048 + 2048],
                                         in_=pb3[:, 0:2048], func=AF.Exp, bias=0.0,
                                         scale=0.0625).then_inc(s_ac, 1)
                            c_ac.inc()
                            P3_EXP[(qc, st_, kb)] = c_ac.v

                    @blk.tensor
                    def _(t, qc=qc, st_=st_):
                        t.wait_ge(s_ac, P3_EXP[(qc, st_, NKB - 1)])
                        for j in range(4):
                            inst = t.matmul(
                                pacc[0:65, j * 512:j * 512 + 512],
                                vaug[:, (NKB - 1) * 65:(NKB - 1) * 65 + 65],
                                expb[:, ((NKB - 1) % 2) * 2048 + j * 512:
                                     ((NKB - 1) % 2) * 2048 + (j + 1) * 512],
                                start=(NKB - 1 == 0), stop=True)
                        inst.then_inc(s_pe, 1)
                        P3_AVD[(qc, st_)] = c_pe.inc()

                    @blk.vector
                    def _(v, qc=qc, st_=st_):
                        v.wait_ge(s_pe, P3_AVD[(qc, st_)])
                        for j in range(4):
                            v.tensor_copy(out=zrow[64:65, j * 512:(j + 1) * 512],
                                          in_=pacc[64:65, j * 512:(j + 1) * 512])
                        v.tensor_copy(out=zrow[64:65, 0:1],
                                      in_=zrow[64:65, 0:1]).then_inc(s_dv, 1)
                        c_dv.inc()
                        P3_ZE[(qc, st_)] = c_dv.v

                    @blk.gpsimd
                    def _(g, qc=qc, st_=st_):
                        g.wait_ge(s_dv, P3_ZE[(qc, st_)])
                        g.wait_ge(s_gp, c_gp.v)
                        g.dma_start(out=rz8[0:4, :],
                                    in_=zrow[64:65, 0:2048]).then_inc(s_gp, 16)
                        c_gp.inc(16)
                        P3_RZ[(qc, st_)] = c_gp.v

                    @blk.vector
                    def _(v, qc=qc, st_=st_):
                        v.wait_ge(s_gp, P3_RZ[(qc, st_)])
                        v.reciprocal(out=rzr[0:4, :], in_=rz8[0:4, :])
                        v.tensor_copy(out=rz8b[0:4, :],
                                      in_=rzr[0:4, :]).then_inc(s_dv, 1)
                        c_dv.inc()
                        P3_RZB[(qc, st_)] = c_dv.v

                    @blk.gpsimd
                    def _(g, qc=qc, st_=st_):
                        g.wait_ge(s_dv, P3_RZB[(qc, st_)])
                        g.wait_ge(s_gp, c_gp.v)
                        g.dma_start(out=rzb[0:1, 0:2048],
                                    in_=rz8b[0:4, 0:512]).then_inc(s_gp, 16)
                        c_gp.inc(16)
                        P3_RZ[(qc, st_, 2)] = c_gp.v

                    for j in range(4):
                        h = 4 * st_ + j
                        it = h // 2

                        @blk.tensor
                        def _(t, qc=qc, st_=st_, j=j):
                            if j == 0:
                                t.wait_ge(s_gp, P3_RZ[(qc, st_, 2)])
                            t.matmul(pb3[0:64, j * 512:(j + 1) * 512],
                                     ones_r[0:1, 0:64],
                                     rzb[0:1, j * 512:(j + 1) * 512],
                                     start=True, stop=True).then_inc(s_pe, 1)
                            P3_PRZ[(qc, st_, j)] = c_pe.inc()

                        @blk.vector
                        def _(v, qc=qc, st_=st_, j=j, h=h, it=it):
                            v.wait_ge(s_pe, P3_PRZ[(qc, st_, j)])
                            if h % 2 == 1 and j == 3:
                                v.wait_ge(s_gp, P3_ODD[(qc, st_, 1)])
                            v.tensor_copy(out=rzbc[0:64, :],
                                          in_=pb3[0:64, j * 512:(j + 1) * 512])
                            if h % 2 == 0:
                                v.tensor_tensor(
                                    out=woin[0:64, it * 512:(it + 1) * 512],
                                    in0=pacc[0:64, j * 512:(j + 1) * 512],
                                    in1=rzbc[0:64, :],
                                    op=OP.mult).then_inc(s_dv, 1)
                            else:
                                v.tensor_tensor(
                                    out=oddst[0:64, 0:512],
                                    in0=pacc[0:64, j * 512:(j + 1) * 512],
                                    in1=rzbc[0:64, :],
                                    op=OP.mult).then_inc(s_dv, 1)
                            c_dv.inc()
                            P3_WOIN[(qc, st_, j)] = c_dv.v

                        if h % 2 == 1:
                            @blk.gpsimd
                            def _(g, qc=qc, st_=st_, j=j, it=it):
                                g.wait_ge(s_dv, P3_WOIN[(qc, st_, j)])
                                g.wait_ge(s_gp, c_gp.v)
                                g.dma_start(out=woin[64:128, it * 512:(it + 1) * 512],
                                            in_=oddst[0:64, 0:512]).then_inc(s_gp, 16)
                                c_gp.inc(16)
                                P3_ODD[(qc, st_, j)] = c_gp.v

                    if st_ == 1:
                        for isub in range(4):
                            tix = qc * 4 + isub
                            base = (isub % 2) * 1024

                            @blk.tensor
                            def _(t, qc=qc, isub=isub, base=base):
                                if isub == 0:
                                    t.wait_ge(s_dv, P3_WOIN[(qc, 1, 3)])
                                    t.wait_ge(s_gp, P3_ODD[(qc, 1, 3)])
                                if isub >= 2:
                                    t.wait_ge(s_ac, P3_EV1[(qc, isub - 2)])
                                for ec in range(2):
                                    for it in range(4):
                                        inst = t.matmul(
                                            pb3[:, base + ec * 512:
                                                base + (ec + 1) * 512],
                                            woin[:, it * 512 + isub * 128:
                                                 it * 512 + (isub + 1) * 128],
                                            wos[:, it * D + ec * 512:
                                                it * D + (ec + 1) * 512],
                                            start=(it == 0), stop=(it == 3))
                                inst.then_inc(s_pe, 1)
                                c_pe.inc()
                                P3_WO[(qc, isub)] = c_pe.v

                            @blk.scalar
                            def _(s, qc=qc, isub=isub, tix=tix, base=base):
                                s.wait_ge(s_pe, P3_WO[(qc, isub)])
                                if tix >= 1:
                                    s.wait_ge(s_in, P3_SP[tix - 1])
                                    s.wait_ge(s_dv, P3_EV2[tix - 1])
                                s.activation(out=att[:, :],
                                             in_=pb3[:, base:base + 1024],
                                             func=AF.Copy, bias=0.0, scale=1.0,
                                             accum_out=colb[:, CB_S1 + tix:
                                                            CB_S1 + tix + 1]
                                             ).then_inc(s_ac, 1)
                                c_ac.inc()
                                P3_EV1[(qc, isub)] = c_ac.v

                            @blk.vector
                            def _(v, qc=qc, isub=isub, tix=tix):
                                v.wait_ge(s_ac, P3_EV1[(qc, isub)])
                                v.scalar_tensor_tensor(
                                    out=attjunk[:, :], in0=att[:, :], scalar=1.0,
                                    op0=OP.mult, in1=att[:, :], op1=OP.mult,
                                    accum_out=colb[:, CB_S2 + tix:CB_S2 + tix + 1]
                                    ).then_inc(s_dv, 1)
                                c_dv.inc()
                                P3_EV2[tix] = c_dv.v

                            @blk.sync
                            def _(sy, tix=tix):
                                sy.wait_ge(s_dv, P3_EV2[tix])
                                sy.dma_start(out=spill_h[tix * 128:(tix + 1) * 128, :],
                                             in_=att[:, :]).then_inc(s_in, 16)
                                c_in.inc(16)
                                P3_SP[tix] = c_in.v
                P3_DONE_DV = c_dv.v
                P3_DONE_AC = c_ac.v
                P3_SP_LAST = c_in.v
                if dbg:
                    @blk.sync
                    def _(sy):
                        sy.wait_ge(s_dv, P3_DONE_DV)
                        sy.wait_ge(s_ac, P3_DONE_AC)
                        sy.dma_start(out=dbg_cb, in_=colb[:, :]).then_inc(s_st, 16)
                        c_st.inc(16)
                        for dst, srcb in ((dbg_xnq, xnq[:, :]), (dbg_qtd, qtd[:, :]),
                                          (dbg_kt, ktdup[:, :]), (dbg_va, vaug[:, :]),
                                          (dbg_iq, invqsb[:, :]),
                                          (dbg_exp, expb[:, 0:2048]),
                                          (dbg_xn2b, xn2b[:, :])):
                            sy.dma_start(out=dst, in_=srcb).then_inc(s_st, 16)
                            c_st.inc(16)

        NT16 = 4 * NQC

        @blk.vector
        def _(v):
            v.wait_ge(s_dv, P3_DONE_DV)
            v.wait_ge(s_ac, P3_DONE_AC)
            s1 = colb[:, CB_S1:CB_S1 + NT16]
            s2 = colb[:, CB_S2:CB_S2 + NT16]
            la = colb[:, CB_LNA:CB_LNA + NT16]
            jk = colb[:, CB_T0:CB_T0 + NT16]
            v.tensor_scalar(out=s1, in0=s1, scalar1=1.0 / D, scalar2=None, op0=OP.mult)
            v.tensor_scalar(out=s2, in0=s2, scalar1=1.0 / D, scalar2=None, op0=OP.mult)
            v.scalar_tensor_tensor(out=la, in0=s1, scalar=1.0, op0=OP.mult,
                                   in1=s1, op1=OP.mult)
            v.tensor_copy(out=jk, in_=s1)
            v.tensor_tensor(out=s2, in0=s2, in1=la, op=OP.subtract)
            v.tensor_copy(out=jk, in_=s1)
            v.tensor_scalar(out=s2, in0=s2, scalar1=EPS, scalar2=None, op0=OP.add)
            v.tensor_copy(out=jk, in_=s1)
            v.reciprocal(out=la, in_=s2).then_inc(s_dv, 1)
        P35_R = c_dv.inc()

        @blk.scalar
        def _(s):
            s.wait_ge(s_dv, P35_R)
            la = colb[:, CB_LNA:CB_LNA + NT16]
            s.activation(out=la, in_=la, func=AF.Sqrt, bias=0.0,
                         scale=1.0).then_inc(s_ac, 1)
        P35_A = c_ac.inc()

        @blk.vector
        def _(v):
            v.wait_ge(s_ac, P35_A)
            s1 = colb[:, CB_S1:CB_S1 + NT16]
            la = colb[:, CB_LNA:CB_LNA + NT16]
            lb = colb[:, CB_LNB:CB_LNB + NT16]
            v.scalar_tensor_tensor(out=lb, in0=s1, scalar=-1.0, op0=OP.mult,
                                   in1=la, op1=OP.mult).then_inc(s_dv, 1)
        P35_DONE = c_dv.inc()

        esB = contextlib.ExitStack()

        def sbB(name, shape, dt=F32):
            return esB.enter_context(nc.sbuf_tensor(name, shape, dt))

        with esB:
            w1q = sbB("w1q", [128, NDT * FF], BF16)
            wsl = sbB("wsl", [128, 2 * 2048])
            wqt = sbB("wqt", [128, 2048])
            zi32w = sbB("zi32w", [128, 2048], I32)
            onwb = sbB("onwb_s", [128, D])
            onbb = sbB("onbb_s", [128, D])
            att = sbB("attB", [128, D])
            att2 = sbB("att2", [128, D])
            fdq = sbB("fdq", [128, D])
            xot = sbB("xotB", [128, 2 * D])
            spl = sbB("spl", [128, 2 * D])
            w2s = sbB("w2s", [128, 2 * D], BF16)
            sl = sbB("sl", [128, NFT * FCW], BF16)
            hsb = sbB("hsb", [128, FCW + 2])
            accv = sbB("accv", [128, FCW])
            accn = sbB("accn", [128, FCW])
            accb = sbB("accb", [128, FCW], BF16)
            acct = sbB("acct", [128, FCW], BF16)
            hq = sbB("hq", [128, 2 * FCW], BF16)
            uq = sbB("uq", [128, FCW])
            zi4 = sbB("zi4", [128, FCW], I32)
            s2bw = sbB("s2bw", [128, FCW])
            srow = sbB("srow", [1, 256], BF16)
            if dbg and T <= 512:
                dbgfd_s = sbB("dbgfd_s", [128, D])
                dbgal_s = sbB("dbgal_s", [128, D])
            pb4 = esB.enter_context(nc.psum_tensor("pb4", [128, 2048], F32))

            @blk.gpsimd
            def _(g):
                g.wait_ge(s_dv, P35_DONE)
                dma_barrier(g)
                g.dma_start(out=onwb[:, :], in_=onw_h[:, :]).then_inc(s_gp, 16)
                c_gp.inc(16)
                g.dma_start(out=onbb[:, :], in_=onb_h[:, :]).then_inc(s_gp, 16)
                c_gp.inc(16)
            PW_ON = c_gp.v

            PW_LD, PW_RS = {}, {}
            pw_list = ([(0, s_) for s_ in range(2 * NDT)] +
                       [(1, s_) for s_ in range(NFT)])
            whs = {0: (w1t_h, 2048), 1: (w2t_h, D)}
            for li, (wi, s_) in enumerate(pw_list):
                wh, slw = whs[wi]
                par = li % 2

                @blk.gpsimd
                def _(g, li=li, wi=wi, s_=s_, par=par, wh=wh, slw=slw):
                    if li == 0:
                        g.wait_ge(s_dv, P35_DONE)
                        dma_barrier(g)
                    if li >= 2:
                        g.wait_ge(s_dv, PW_RS[li - 2])
                    sem, ctr = (s_gx, c_gx) if li % 2 == 0 else (s_gx2, c_gx2)
                    g.dma_start(out=wsl[:, par * 2048:par * 2048 + slw],
                                in_=wh[:, s_ * slw:(s_ + 1) * slw]).then_inc(sem, 16)
                    ctr.inc(16)
                    PW_LD[li] = ctr.v

                @blk.vector
                def _(v, li=li, wi=wi, s_=s_, par=par, slw=slw):
                    v.wait_ge(s_gx if li % 2 == 0 else s_gx2, PW_LD[li])
                    if s_ == 0:
                        v.reduce_sum(out=colb[:, CB_T0 + wi:CB_T0 + wi + 1],
                                     in_=wsl[:, par * 2048:par * 2048 + slw],
                                     axis=AX.X,
                                     apply_absolute_value=True).then_inc(s_dv, 1)
                    else:
                        v.reduce_sum(out=colb[:, CB_T1:CB_T1 + 1],
                                     in_=wsl[:, par * 2048:par * 2048 + slw],
                                     axis=AX.X, apply_absolute_value=True)
                        v.tensor_copy(out=colb[:, CB_T1 + 2:CB_T1 + 3],
                                      in_=colb[:, CB_M1:CB_M1 + 1])
                        v.tensor_tensor(out=colb[:, CB_T0 + wi:CB_T0 + wi + 1],
                                        in0=colb[:, CB_T0 + wi:CB_T0 + wi + 1],
                                        in1=colb[:, CB_T1:CB_T1 + 1],
                                        op=OP.add).then_inc(s_dv, 1)
                    c_dv.inc()
                    PW_RS[li] = c_dv.v
            PW_A = c_dv.v

            @blk.gpsimd
            def _(g):
                g.wait_ge(s_dv, PW_A)
                g.dma_start(out=scr_h[512:640],
                            in_=colb[:, CB_T0:CB_T0 + 1]).then_inc(s_gp, 16)
                c_gp.inc(16)
                g.dma_start(out=scr_h[640:768],
                            in_=colb[:, CB_T0 + 1:CB_T0 + 2]).then_inc(s_gp, 16)
                c_gp.inc(16)
                g.wait_ge(s_gp, c_gp.v)
                g.dma_start(out=scst[0:1, 64:192],
                            in_=scr_h[512:640]).then_inc(s_gp, 16)
                c_gp.inc(16)
                g.dma_start(out=scst[0:1, 192:320],
                            in_=scr_h[640:768]).then_inc(s_gp, 16)
                c_gp.inc(16)
            PW_ROWS = c_gp.v

            @blk.vector
            def _(v):
                v.wait_ge(s_gp, PW_ROWS)
                v.reduce_sum(out=scst[0:1, 56:57], in_=scst[0:1, 64:192], axis=AX.X)
                v.reduce_sum(out=scst[0:1, 57:58], in_=scst[0:1, 192:320], axis=AX.X)
                v.tensor_copy(out=scst[0:1, 60:61], in_=scst[0:1, 64:65])
                v.tensor_scalar(out=scst[0:1, 56:58], in0=scst[0:1, 56:58],
                                scalar1=1.0 / (FF * D), scalar2=None, op0=OP.mult)
                v.tensor_copy(out=scst[0:1, 60:61], in_=scst[0:1, 64:65])
                v.tensor_scalar(out=scst[0:1, 56:58], in0=scst[0:1, 56:58],
                                scalar1=1e-5, scalar2=None, op0=OP.max)
                v.tensor_copy(out=scst[0:1, 60:61], in_=scst[0:1, 64:65])
                v.reciprocal(out=scst[0:1, 58:60], in_=scst[0:1, 56:58])
                v.tensor_copy(out=scstb[0:1, 12:14], in_=scst[0:1, 56:58])
                v.tensor_copy(out=scstb[0:1, 14:16],
                              in_=scst[0:1, 58:60]).then_inc(s_dv, 1)
            PW_SC = c_dv.inc()

            @blk.tensor
            def _(t):
                t.wait_ge(s_dv, PW_SC)
                t.matmul(pb4[:, 0:1], ones_r[0:1, :], scstb[0:1, 12:13], start=True,
                         stop=True)
                t.matmul(pb4[:, 1:2], ones_r[0:1, :], scstb[0:1, 13:14], start=True,
                         stop=True)
                t.matmul(pb4[:, 2:3], ones_r[0:1, :], scstb[0:1, 14:15], start=True,
                         stop=True)
                t.matmul(pb4[:, 3:4], ones_r[0:1, :], scstb[0:1, 15:16], start=True,
                         stop=True).then_inc(s_pe, 1)
            PW_REP = c_pe.inc()

            @blk.vector
            def _(v):
                v.wait_ge(s_pe, PW_REP)
                v.tensor_copy(out=colb[:, CB_M1BAR:CB_M1BAR + 1], in_=pb4[:, 0:1])
                v.tensor_scalar(out=colb[:, CB_M2B127:CB_M2B127 + 1], in0=pb4[:, 1:2],
                                scalar1=1.0 / 127.0, scalar2=None, op0=OP.mult)
                v.tensor_copy(out=colb[:, CB_T0:CB_T0 + 1], in_=pb4[:, 2:3])
                v.tensor_copy(out=colb[:, CB_T1 + 1:CB_T1 + 2], in_=pb4[:, 3:4])
                v.tensor_scalar(out=cwc[:, :], in0=cwc[:, :],
                                scalar1=colb[:, CB_M1BAR:CB_M1BAR + 1], scalar2=None,
                                op0=OP.mult).then_inc(s_dv, 1)
            PW_COLS = c_dv.inc()

            PW_QLD, PW_QT, PW_QST = {}, {}, {}
            for li, (wi, s_) in enumerate(pw_list):
                wh, slw = whs[wi]
                par = li % 2

                @blk.gpsimd
                def _(g, li=li, wi=wi, s_=s_, par=par, wh=wh, slw=slw):
                    if li == 0:
                        g.wait_ge(s_dv, PW_COLS)
                    if li >= 2:
                        g.wait_ge(s_dv, PW_QT[li - 2])
                    sem, ctr = (s_gx, c_gx) if li % 2 == 0 else (s_gx2, c_gx2)
                    g.dma_start(out=wsl[:, par * 2048:par * 2048 + slw],
                                in_=wh[:, s_ * slw:(s_ + 1) * slw]).then_inc(sem, 16)
                    ctr.inc(16)
                    PW_QLD[li] = ctr.v

                @blk.vector
                def _(v, li=li, wi=wi, s_=s_, par=par, slw=slw):
                    v.wait_ge(s_gx if li % 2 == 0 else s_gx2, PW_QLD[li])
                    if wi == 1 and s_ >= 2:
                        v.wait_ge(s_gp, PW_QST[s_ - 2])

                    scol = colb[:, CB_T0 + wi:CB_T0 + wi + 1] if wi == 0 else \
                        colb[:, CB_T1 + 1:CB_T1 + 2]
                    src = wsl[:, par * 2048:par * 2048 + slw]
                    v.tensor_scalar(out=wqt[:, 0:slw], in0=src, scalar1=scol,
                                    scalar2=128.5, op0=OP.mult, op1=OP.add)
                    v.tensor_copy(out=zi32w[:, 0:slw], in_=wqt[:, 0:slw])
                    v.tensor_copy(out=pb4[:, 0:slw], in_=zi32w[:, 0:slw])
                    v.tensor_tensor(out=sl[:, 0:slw], in0=pb4[:, 0:slw],
                                    in1=wqt[:, 0:slw], op=OP.is_le)
                    v.tensor_tensor(out=pb4[:, 0:slw], in0=pb4[:, 0:slw],
                                    in1=sl[:, 0:slw], op=OP.add)
                    v.tensor_scalar(out=wqt[:, 0:slw], in0=pb4[:, 0:slw],
                                    scalar1=-129.0, scalar2=1.0, op0=OP.add,
                                    op1=OP.min)
                    if wi == 0:
                        v.tensor_scalar(out=w1q[:, s_ * 2048:(s_ + 1) * 2048],
                                        in0=wqt[:, 0:slw], scalar1=-1.0, scalar2=None,
                                        op0=OP.max).then_inc(s_dv, 1)
                    else:
                        v.tensor_scalar(out=w2s[:, (s_ % 2) * D:(s_ % 2) * D + D],
                                        in0=wqt[:, 0:slw], scalar1=-1.0, scalar2=None,
                                        op0=OP.max).then_inc(s_dv, 1)
                    c_dv.inc()
                    PW_QT[li] = c_dv.v

                if wi == 1:
                    @blk.gpsimd
                    def _(g, li=li, s_=s_):
                        g.wait_ge(s_dv, PW_QT[li])
                        g.wait_ge(s_gp, c_gp.v)
                        g.dma_start(out=w2qh[:, s_ * D:(s_ + 1) * D],
                                    in_=w2s[:, (s_ % 2) * D:(s_ % 2) * D + D]
                                    ).then_inc(s_gp, 16)
                        c_gp.inc(16)
                        PW_QST[s_] = c_gp.v
            PW_DONE_GP = c_gp.v

            P4_W2, P4_M1, P4_CV, P4_SL, P4_S2A, P4_S2B = {}, {}, {}, {}, {}, {}
            P4_S2C, P4_HQ, P4_M2, P4_FD, P4_LN, P4_AD, P4_OUT = ({} for _ in range(7))
            for fc in range(NFC):
                for ft in range(NFT):
                    phb = ft % 2

                    @blk.tensor
                    def _(t, fc=fc, ft=ft, phb=phb):
                        if fc == 0 and ft == 0:
                            t.wait_ge(s_gp, PW_DONE_GP)
                            t.wait_ge(s_dv, P2_TRXC[NTT - 1])
                        key = (fc, ft - 2) if ft >= 2 else \
                            ((fc - 1, NFT - 2 + ft) if fc > 0 else None)
                        if key is not None:
                            t.wait_ge(s_dv, P4_CV[key])
                        for dt in range(NDT):
                            inst = t.matmul(
                                pb4[:, phb * 512:phb * 512 + FCW + 2],
                                w1q[:, dt * FF + ft * 128:dt * FF + (ft + 1) * 128],
                                xnq[:, dt * (T + 2) + fc * FCW:
                                    dt * (T + 2) + fc * FCW + FCW + 2],
                                start=(dt == 0), stop=(dt == NDT - 1))
                        inst.then_inc(s_pe, 1)
                        c_pe.inc()
                        P4_M1[(fc, ft)] = c_pe.v

                    @blk.vector
                    def _(v, fc=fc, ft=ft, phb=phb):
                        v.wait_ge(s_pe, P4_M1[(fc, ft)])
                        if ft >= 1:
                            v.wait_ge(s_ac, P4_S2A[(fc, ft - 1)])
                        elif fc > 0:
                            v.wait_ge(s_ac, P4_S2A[(fc - 1, NFT - 1)])
                        v.tensor_tensor(out=hsb[:, :],
                                        in0=pb4[:, phb * 512:phb * 512 + FCW + 2],
                                        in1=invqsb[:, fc * FCW:fc * FCW + FCW + 2],
                                        op=OP.mult).then_inc(s_dv, 1)
                        c_dv.inc()
                        P4_CV[(fc, ft)] = c_dv.v
                        w0 = cwc[:, 3 * ft:3 * ft + 1]
                        w1c = cwc[:, 3 * ft + 1:3 * ft + 2]
                        w2c = cwc[:, 3 * ft + 2:3 * ft + 3]
                        v.tensor_scalar(out=uq[:, :], in0=hsb[:, 0:FCW], scalar1=w0,
                                        scalar2=None, op0=OP.mult)
                        v.scalar_tensor_tensor(out=uq[:, :], in0=hsb[:, 1:FCW + 1],
                                               scalar=w1c, op0=OP.mult, in1=uq[:, :],
                                               op1=OP.add)
                        v.scalar_tensor_tensor(out=uq[:, :], in0=hsb[:, 2:FCW + 2],
                                               scalar=w2c, op0=OP.mult, in1=uq[:, :],
                                               op1=OP.add).then_inc(s_dv, 1)
                        c_dv.inc()
                        P4_SL[(fc, ft)] = c_dv.v

                    @blk.scalar
                    def _(s, fc=fc, ft=ft):
                        s.wait_ge(s_dv, P4_SL[(fc, ft)])
                        s.activation(out=sl[:, ft * FCW:(ft + 1) * FCW], in_=uq[:, :],
                                     func=AF.Silu, bias=cbc[:, ft:ft + 1],
                                     scale=1.0).then_inc(s_ac, 1)
                        c_ac.inc()
                        P4_S2A[(fc, ft)] = c_ac.v

                    @blk.vector
                    def _(v, fc=fc, ft=ft):
                        v.wait_ge(s_ac, P4_S2A[(fc, ft)])
                        if ft == 0:
                            v.tensor_copy(out=accv[:, :], in_=sl[:, 0:FCW])
                            v.tensor_copy(out=accn[:, :],
                                          in_=sl[:, 0:FCW]).then_inc(s_dv, 1)
                        else:
                            v.tensor_tensor(out=accv[:, :], in0=accv[:, :],
                                            in1=sl[:, ft * FCW:(ft + 1) * FCW],
                                            op=OP.max)
                            v.tensor_tensor(out=accn[:, :], in0=accn[:, :],
                                            in1=sl[:, ft * FCW:(ft + 1) * FCW],
                                            op=OP.min).then_inc(s_dv, 1)
                        c_dv.inc()
                        P4_S2B[(fc, ft)] = c_dv.v

                @blk.vector
                def _(v, fc=fc):
                    v.wait_ge(s_dv, P4_S2B[(fc, NFT - 1)])
                    v.tensor_scalar(out=accn[:, :], in0=accn[:, :], scalar1=-1.0,
                                    scalar2=None, op0=OP.mult)
                    v.tensor_tensor(out=accv[:, :], in0=accv[:, :], in1=accn[:, :],
                                    op=OP.max)
                    v.tensor_copy(out=accb[:, :], in_=accv[:, :]).then_inc(s_dv, 1)
                    c_dv.inc()
                    P4_S2C[(fc, 0)] = c_dv.v

                @blk.scalar
                def _(s, fc=fc):
                    s.wait_ge(s_dv, P4_S2C[(fc, 0)])
                    s.dma_start_transpose(out=acct[:, 0:128],
                                          in_=accb[:, 0:128]).then_inc(s_tr, 16)
                    c_tr.inc(16)
                    s.dma_start_transpose(out=acct[:, 128:256],
                                          in_=accb[:, 128:256]).then_inc(s_tr, 16)
                    c_tr.inc(16)
                    P4_S2C[(fc, "tr")] = c_tr.v

                @blk.vector
                def _(v, fc=fc):
                    v.wait_ge(s_tr, P4_S2C[(fc, "tr")])
                    v.reduce_max(out=colb[:, CB_T0 + 2:CB_T0 + 3], in_=acct[:, 0:128],
                                 axis=AX.X)
                    v.reduce_max(out=colb[:, CB_T0 + 3:CB_T0 + 4],
                                 in_=acct[:, 128:256], axis=AX.X)
                    v.tensor_copy(out=colpad[:, 2:3],
                                  in_=colb[:, CB_M1:CB_M1 + 1])
                    v.tensor_scalar(out=colb[:, CB_T0 + 2:CB_T0 + 4],
                                    in0=colb[:, CB_T0 + 2:CB_T0 + 4], scalar1=1e-5,
                                    scalar2=None, op0=OP.max)
                    v.tensor_copy(out=colpad[:, 2:3],
                                  in_=colb[:, CB_M1:CB_M1 + 1])
                    v.tensor_scalar(out=colb[:, CB_G2W + 2 * fc:CB_G2W + 2 * fc + 2],
                                    in0=colb[:, CB_T0 + 2:CB_T0 + 4],
                                    scalar1=colb[:, CB_M2B127:CB_M2B127 + 1],
                                    scalar2=None, op0=OP.mult)
                    v.reciprocal(out=colb[:, CB_T1 + 2:CB_T1 + 4],
                                 in_=colb[:, CB_T0 + 2:CB_T0 + 4])
                    v.tensor_copy(out=colpad[:, 2:3],
                                  in_=colb[:, CB_M1:CB_M1 + 1])
                    v.tensor_scalar(out=colb[:, CB_T1 + 2:CB_T1 + 4],
                                    in0=colb[:, CB_T1 + 2:CB_T1 + 4], scalar1=127.0,
                                    scalar2=None, op0=OP.mult)
                    v.tensor_copy(out=colpad[:, 2:3],
                                  in_=colb[:, CB_M1:CB_M1 + 1])
                    v.tensor_copy(out=colpad[:, 0:2],
                                  in_=colb[:, CB_T1 + 2:CB_T1 + 4]).then_inc(s_dv, 1)
                    c_dv.inc()
                    P4_S2C[(fc, 1)] = c_dv.v

                @blk.scalar
                def _(s, fc=fc):
                    s.wait_ge(s_dv, P4_S2C[(fc, 1)])
                    s.dma_start_transpose(out=rowt[:, :],
                                          in_=colpad[:, :]).then_inc(s_tr, 16)
                    c_tr.inc(16)
                    P4_S2C[(fc, "t2")] = c_tr.v

                @blk.gpsimd
                def _(g, fc=fc):
                    g.wait_ge(s_tr, P4_S2C[(fc, "t2")])
                    g.dma_start(out=srow[0:1, :],
                                in_=rowt[0:2, 0:128]).then_inc(s_gp, 16)
                    c_gp.inc(16)
                    P4_S2C[(fc, "rw")] = c_gp.v

                @blk.tensor
                def _(t, fc=fc):
                    t.wait_ge(s_gp, P4_S2C[(fc, "rw")])
                    t.matmul(pb4[:, 1024:1152], ones_r[0:1, :], srow[0:1, 0:128],
                             start=True, stop=True)
                    t.matmul(pb4[:, 1152:1280], ones_r[0:1, :], srow[0:1, 128:256],
                             start=True, stop=True).then_inc(s_pe, 1)
                    c_pe.inc()
                    P4_S2C[(fc, 2)] = c_pe.v

                @blk.vector
                def _(v, fc=fc):
                    v.wait_ge(s_pe, P4_S2C[(fc, 2)])
                    v.tensor_copy(out=s2bw[:, 0:256],
                                  in_=pb4[:, 1024:1280]).then_inc(s_dv, 1)
                    c_dv.inc()
                    P4_S2C[(fc, 3)] = c_dv.v

                for ft in range(NFT):
                    @blk.sync
                    def _(sy, fc=fc, ft=ft):
                        if fc == 0 and ft == 0:
                            sy.wait_ge(s_gp, PW_DONE_GP)
                        key = (fc, ft - 2) if ft >= 2 else \
                            ((fc - 1, NFT - 2 + ft) if fc > 0 else None)
                        if key is not None:
                            sy.wait_ge(s_pe, P4_M2[key])
                        sem, ctr = (s_in, c_in) if ft % 2 == 0 else (s_i2, c_i2)
                        sy.dma_start(out=w2s[:, (ft % 2) * D:(ft % 2 + 1) * D],
                                     in_=w2qh[:, ft * D:(ft + 1) * D]).then_inc(sem, 16)
                        ctr.inc(16)
                        P4_W2[(fc, ft)] = ctr.v

                    @blk.vector
                    def _(v, fc=fc, ft=ft):
                        if ft == 0:
                            v.wait_ge(s_dv, P4_S2C[(fc, 3)])
                        if ft >= 2:
                            v.wait_ge(s_pe, P4_M2[(fc, ft - 2)])
                        v.tensor_tensor(out=uq[:, :],
                                        in0=sl[:, ft * FCW:(ft + 1) * FCW],
                                        in1=s2bw[:, 0:FCW], op=OP.mult)
                        v.tensor_scalar(out=uq[:, :], in0=uq[:, :], scalar1=128.5,
                                        scalar2=None, op0=OP.add)
                        v.tensor_copy(out=zi4[:, :], in_=uq[:, :])
                        v.tensor_copy(out=accv[:, :], in_=zi4[:, :])
                        v.tensor_tensor(out=accb[:, :], in0=accv[:, :],
                                        in1=uq[:, :], op=OP.is_le)
                        v.tensor_tensor(out=accv[:, :], in0=accv[:, :],
                                        in1=accb[:, :], op=OP.add)
                        v.tensor_scalar(out=hq[:, (ft % 2) * FCW:(ft % 2 + 1) * FCW],
                                        in0=accv[:, :], scalar1=-129.0, scalar2=127.0,
                                        op0=OP.add, op1=OP.min).then_inc(s_dv, 1)
                        c_dv.inc()
                        P4_HQ[(fc, ft)] = c_dv.v

                    @blk.tensor
                    def _(t, fc=fc, ft=ft):
                        t.wait_ge(s_dv, P4_HQ[(fc, ft)])
                        t.wait_ge(s_in if ft % 2 == 0 else s_i2, P4_W2[(fc, ft)])
                        if ft == 0 and fc > 0:
                            t.wait_ge(s_ac, P4_FD[(fc - 1) * 2 + 1])
                        for isub in range(2):
                            for ec in range(2):
                                inst = t.matmul(
                                    pacc[:, (isub * 2 + ec) * 512:
                                         (isub * 2 + ec + 1) * 512],
                                    hq[:, (ft % 2) * FCW + isub * 128:
                                       (ft % 2) * FCW + (isub + 1) * 128],
                                    w2s[:, (ft % 2) * D + ec * 512:
                                        (ft % 2) * D + (ec + 1) * 512],
                                    start=(ft == 0), stop=(ft == NFT - 1))
                        inst.then_inc(s_pe, 1)
                        c_pe.inc()
                        P4_M2[(fc, ft)] = c_pe.v

                if dbg and fc == 0:
                    @blk.sync
                    def _(sy):
                        sy.wait_ge(s_pe, P4_M2[(0, NFT - 1)])
                        sy.dma_start(out=dbg_sl, in_=sl[:, :]).then_inc(s_st, 16)
                        c_st.inc(16)
                        sy.dma_start(out=dbg_w1q, in_=w1q[:, :]).then_inc(s_st, 16)
                        c_st.inc(16)
                        sy.dma_start(out=dbg_s2, in_=s2bw[:, :]).then_inc(s_st, 16)
                        c_st.inc(16)
                        sy.dma_start(out=dbg_hq, in_=hq[:, :]).then_inc(s_st, 16)
                        c_st.inc(16)

                for isub in range(2):
                    tix = fc * 2 + isub

                    @blk.gpsimd
                    def _(g, fc=fc, isub=isub, tix=tix):
                        if tix == 0:
                            g.wait_ge(s_in, P3_SP_LAST)
                            dma_barrier(g)
                        if tix >= 2:
                            g.wait_ge(s_dv, P4_AD[tix - 2])
                        sem, ctr = (s_gx, c_gx) if tix % 2 == 0 else (s_gx2, c_gx2)
                        g.dma_start(out=spl[:, (tix % 2) * D:(tix % 2 + 1) * D],
                                    in_=spill_h[tix * 128:(tix + 1) * 128, :]
                                    ).then_inc(sem, 16)
                        ctr.inc(16)
                        g.dma_start(out=xot[:, (tix % 2) * D:(tix % 2 + 1) * D],
                                    in_=xow_h[tix * 128:(tix + 1) * 128, :]
                                    ).then_inc(sem, 16)
                        ctr.inc(16)
                        P4_LN[tix] = ctr.v

                    @blk.scalar
                    def _(s, fc=fc, isub=isub, tix=tix):
                        s.wait_ge(s_pe, P4_M2[(fc, NFT - 1)])
                        s.wait_ge(s_gx if tix % 2 == 0 else s_gx2, P4_LN[tix])
                        if tix >= 1:
                            s.wait_ge(s_dv, P4_AD[tix - 1])
                            s.wait_ge(s_st, P4_OUT[tix - 1])
                        s.activation(out=fdq[:, :],
                                     in_=pacc[:, isub * 1024:(isub + 1) * 1024],
                                     func=AF.Copy, bias=0.0, scale=1.0)
                        s.activation(out=att[:, :],
                                     in_=spl[:, (tix % 2) * D:(tix % 2 + 1) * D],
                                     func=AF.Identity,
                                     bias=colb[:, CB_LNB + tix:CB_LNB + tix + 1],
                                     scale=colb[:, CB_LNA + tix:CB_LNA + tix + 1]
                                     ).then_inc(s_ac, 1)
                        c_ac.inc()
                        P4_FD[tix] = c_ac.v

                    @blk.vector
                    def _(v, fc=fc, isub=isub, tix=tix):
                        v.wait_ge(s_ac, P4_FD[tix])
                        if tix >= 1:
                            v.wait_ge(s_st, P4_OUT[tix - 1])
                        g2w = colb[:, CB_G2W + 2 * fc + isub:
                                   CB_G2W + 2 * fc + isub + 1]
                        v.tensor_scalar(out=fdq[:, :], in0=fdq[:, :], scalar1=g2w,
                                        scalar2=None, op0=OP.mult)
                        if dbg and T <= 512 and tix == 0:
                            v.tensor_copy(out=dbgfd_s[:, :], in_=fdq[:, :])
                            v.tensor_copy(out=dbgal_s[:, :], in_=att[:, :])
                        v.tensor_tensor(out=att2[:, :], in0=att[:, :], in1=onwb[:, :],
                                        op=OP.mult)
                        v.tensor_tensor(out=att2[:, :], in0=att2[:, :], in1=onbb[:, :],
                                        op=OP.add)
                        v.tensor_tensor(out=att2[:, :], in0=att2[:, :], in1=fdq[:, :],
                                        op=OP.add)
                        v.tensor_tensor(out=att[:, :], in0=att2[:, :],
                                        in1=xot[:, (tix % 2) * D:(tix % 2 + 1) * D],
                                        op=OP.add).then_inc(s_dv, 1)
                        c_dv.inc()
                        P4_AD[tix] = c_dv.v

                    @blk.sync
                    def _(sy, tix=tix):
                        sy.wait_ge(s_dv, P4_AD[tix])
                        sy.dma_start(out=y_h[tix * 128:(tix + 1) * 128, :],
                                     in_=att[:, :]).then_inc(s_st, 16)
                        c_st.inc(16)
                        P4_OUT[tix] = c_st.v

            if dbg and T <= 512:
                @blk.sync
                def _(sy):
                    sy.wait_ge(s_dv, P4_AD[0] if 0 in P4_AD else c_dv.v)
                    sy.dma_start(out=dbg_fd, in_=dbgfd_s[:, :]).then_inc(s_st, 16)
                    c_st.inc(16)
                    sy.dma_start(out=dbg_al, in_=dbgal_s[:, :]).then_inc(s_st, 16)
                    c_st.inc(16)
            if dbg:
                @blk.sync
                def _(sy):
                    sy.wait_ge(s_dv, c_dv.v)
                    sy.dma_start(out=dbg_scst, in_=scst[0:1, :]).then_inc(s_st, 16)
                    c_st.inc(16)
                    sy.dma_start(out=dbg_cb2, in_=colb[:, :]).then_inc(s_st, 16)
                    c_st.inc(16)

            @blk.sync
            def _(sy):
                sy.wait_ge(s_st, c_st.v)
    return nc


def _tile_pd(a, p=128):
    n = a.shape[0] // p
    return np.ascontiguousarray(
        a.reshape(n, p, a.shape[1]).transpose(1, 0, 2).reshape(p, n * a.shape[1]))


def kernel(L0_attn_x, L0_attn_Wq, L0_attn_Wkv, L0_attn_Wo, L0_attn_norm_w,
           L0_attn_norm_b, L0_attn_out_norm_w, L0_attn_out_norm_b, L0_ffn_w1,
           L0_ffn_conv_w, L0_ffn_conv_b, L0_ffn_w2):
    x = np.asarray(L0_attn_x, dtype=np.float32)
    B, NB, _ = x.shape
    T = NB // 4
    nc = build(NB, T)

    f32 = np.float32
    wq_s = _tile_pd(np.asarray(L0_attn_Wq, f32))
    wkv_s = _tile_pd(np.asarray(L0_attn_Wkv, f32))
    wo_s = _tile_pd(np.asarray(L0_attn_Wo, f32))
    w1t_s = _tile_pd(np.ascontiguousarray(np.asarray(L0_ffn_w1, f32).T))
    w2t_s = _tile_pd(np.ascontiguousarray(np.asarray(L0_ffn_w2, f32).T))
    cw_s = _tile_pd(np.ascontiguousarray(np.asarray(L0_ffn_conv_w, f32)[:, 0, :]))
    cb_s = _tile_pd(np.asarray(L0_ffn_conv_b, f32).reshape(-1, 1))
    nw_s = _tile_pd(np.asarray(L0_attn_norm_w, f32).reshape(-1, 1))
    nb_s = _tile_pd(np.asarray(L0_attn_norm_b, f32).reshape(-1, 1))
    nwb_s = np.ascontiguousarray(np.broadcast_to(np.asarray(L0_attn_norm_w, f32), (128, 1024)))
    nbb_s = np.ascontiguousarray(np.broadcast_to(np.asarray(L0_attn_norm_b, f32), (128, 1024)))
    onw_s = np.ascontiguousarray(np.broadcast_to(np.asarray(L0_attn_out_norm_w, f32), (128, 1024)))
    onb_s = np.ascontiguousarray(np.broadcast_to(np.asarray(L0_attn_out_norm_b, f32), (128, 1024)))

    in_maps = []
    for c in range(8):
        b, qo = c // 4, c % 4
        r0 = qo * T
        xb_ = x[b]
        rot = np.concatenate([xb_[r0:], xb_[:r0]], axis=0)
        im = {
            "xT": np.ascontiguousarray(rot.T),
            "xown": np.ascontiguousarray(xb_[r0:r0 + T]),
            "wq": wq_s, "wkv": wkv_s, "wo": wo_s,
            "w1t": w1t_s, "w2t": w2t_s,
            "convw": cw_s, "convb": cb_s, "nw": nw_s, "nb": nb_s,
            "nwb": nwb_s, "nbb": nbb_s, "onwb": onw_s, "onbb": onb_s,
            "hmask": np.ascontiguousarray(
                np.broadcast_to(np.array([[1.0 if qo > 0 else 0.0,
                                           1.0 if qo < 3 else 0.0]], f32),
                                (128, 2))),
        }
        in_maps.append(im)

    res = run_bass_kernel_spmd(nc, in_maps, core_ids=list(range(8)))
    out = np.empty((B, NB, 1024), dtype=np.float32)
    for c in range(8):
        b, qo = c // 4, c % 4
        out[b, qo * T:(qo + 1) * T] = res.results[c]["y"]
    return out


# version 07 (final)
# speedup vs baseline: 1.1951x; 1.1951x over previous
import contextlib
import numpy as np
import ml_dtypes
import concourse.bass as bass
import concourse.mybir as mybir
from concourse.bass_utils import run_bass_kernel_spmd

F32 = mybir.dt.float32
BF16 = mybir.dt.bfloat16
I32 = mybir.dt.int32
AF = mybir.ActivationFunctionType
OP = mybir.AluOpType
AX = mybir.AxisListType
BF = ml_dtypes.bfloat16

D = 1024
FF = 4096
NDT = D // 128
NFT = FF // 128
EPS = 1e-5


def build(NB, T, dbg=False):
    NKB = NB // 128
    NSC = NB // 512
    NQC = T // 512
    NTT = T // 128
    NFC = T // 256
    FCW = 256
    assert T % 512 == 0 and NB % 512 == 0 and NSC <= 16 and NFC <= 8

    nc = bass.Bass("TRN2", target_bir_lowering=False,
                   detect_race_conditions=False)

    xT_h = nc.dram_tensor("xT", [D, NB], F32, kind="ExternalInput").ap()
    xow_h = nc.dram_tensor("xown", [T, D], F32, kind="ExternalInput").ap()
    wq_h = nc.dram_tensor("wq", [128, NDT * 512], F32, kind="ExternalInput").ap()
    wkv_h = nc.dram_tensor("wkv", [128, NDT * 128], F32, kind="ExternalInput").ap()
    wo_h = nc.dram_tensor("wo", [128, 4 * D], F32, kind="ExternalInput").ap()
    w1t_h = nc.dram_tensor("w1t", [128, NDT * FF], F32, kind="ExternalInput").ap()
    w2t_h = nc.dram_tensor("w2t", [128, NFT * D], F32, kind="ExternalInput").ap()
    cw_h = nc.dram_tensor("convw", [128, 3 * NFT], F32, kind="ExternalInput").ap()
    cb_h = nc.dram_tensor("convb", [128, NFT], F32, kind="ExternalInput").ap()
    nw_h = nc.dram_tensor("nw", [128, NDT], F32, kind="ExternalInput").ap()
    nb_h = nc.dram_tensor("nb", [128, NDT], F32, kind="ExternalInput").ap()
    nwb_h = nc.dram_tensor("nwb", [128, D], F32, kind="ExternalInput").ap()
    nbb_h = nc.dram_tensor("nbb", [128, D], F32, kind="ExternalInput").ap()
    onw_h = nc.dram_tensor("onwb", [128, D], F32, kind="ExternalInput").ap()
    onb_h = nc.dram_tensor("onbb", [128, D], F32, kind="ExternalInput").ap()
    hm_h = nc.dram_tensor("hmask", [128, 2], F32, kind="ExternalInput").ap()
    y_h = nc.dram_tensor("y", [T, D], F32, kind="ExternalOutput").ap()
    scr_h = nc.dram_tensor("scr", [4 * NB + 8192], F32, kind="Internal").ap()
    spill_h = nc.dram_tensor("spill", [T, D], F32, kind="Internal").ap()
    w2qh = nc.dram_tensor("w2qh", [128, NFT * D], BF16, kind="Internal").ap()
    if dbg:
        dbg_xnq = nc.dram_tensor("dbg_xnq", [128, NDT * (T + 2)], BF16, kind="ExternalOutput").ap()
        dbg_qtd = nc.dram_tensor("dbg_qtd", [128, 8 * T], BF16, kind="ExternalOutput").ap()
        dbg_kt = nc.dram_tensor("dbg_kt", [128, NB], BF16, kind="ExternalOutput").ap()
        dbg_va = nc.dram_tensor("dbg_va", [128, 65 * NKB], BF16, kind="ExternalOutput").ap()
        dbg_iq = nc.dram_tensor("dbg_iq", [128, T + 2], BF16, kind="ExternalOutput").ap()
        dbg_sl = nc.dram_tensor("dbg_sl", [128, NFT * 256], BF16, kind="ExternalOutput").ap()
        dbg_w1q = nc.dram_tensor("dbg_w1q", [128, NDT * FF], BF16, kind="ExternalOutput").ap()
        dbg_exp = nc.dram_tensor("dbg_exp", [128, 2048], BF16, kind="ExternalOutput").ap()
        dbg_s2 = nc.dram_tensor("dbg_s2", [128, 256], F32, kind="ExternalOutput").ap()
        dbg_hq = nc.dram_tensor("dbg_hq", [128, 512], BF16, kind="ExternalOutput").ap()
        dbg_fd = nc.dram_tensor("dbg_fd", [128, 1024], F32, kind="ExternalOutput").ap()
        dbg_cb = nc.dram_tensor("dbg_cb", [128, 224], F32, kind="ExternalOutput").ap()
        dbg_al = nc.dram_tensor("dbg_al", [128, 1024], F32, kind="ExternalOutput").ap()
        dbg_xn2b = nc.dram_tensor("dbg_xn2b", [128, NDT * 512], BF16, kind="ExternalOutput").ap()
        dbg_scst = nc.dram_tensor("dbg_scst", [1, 320], F32, kind="ExternalOutput").ap()
        dbg_cb2 = nc.dram_tensor("dbg_cb2", [128, 224], F32, kind="ExternalOutput").ap()

    es = contextlib.ExitStack()

    def sb(name, shape, dt=F32):
        return es.enter_context(nc.sbuf_tensor(name, shape, dt))

    with es:
        xnq = sb("xnq", [128, NDT * (T + 2)], BF16)
        invqsb = sb("invqsb", [128, T + 2], BF16)
        cwc = sb("cwc", [128, 3 * NFT])
        cbc = sb("cbc", [128, NFT])
        nwc = sb("nwc", [128, NDT])
        nbc = sb("nbc", [128, NDT])
        hmask = sb("hmask_s", [128, 2])
        ones_r = sb("ones_r", [1, 128], BF16)
        ones_c = sb("ones_c", [128, 1], BF16)
        fatb = sb("fatb", [16, 2 * 512], BF16)
        colb = sb("colb", [128, 224])
        colbb = sb("colbb", [128, 16], BF16)
        scst = sb("scst", [1, 320])
        scstb = sb("scstb", [1, 16], BF16)
        halos = sb("halos", [128, 40])
        rowt = sb("rowt", [128, 128], BF16)
        colpad = sb("colpad", [128, 128], BF16)

        CB_VB, CB_U, CB_NEGU, CB_M1BAR, CB_M2B127 = 0, 1, 2, 3, 4
        CB_M1, CB_A1, CB_A, CB_G, CB_MX = 16, 32, 48, 64, 80
        CB_T0, CB_T1, CB_IQ = 96, 112, 128
        CB_S1, CB_S2, CB_LNA, CB_LNB = 144, 160, 176, 192
        CB_G2W = 208

        pacc = es.enter_context(nc.psum_tensor("pacc", [128, 2048], F32))

        s_in = es.enter_context(nc.semaphore("s_in"))
        s_i2 = es.enter_context(nc.semaphore("s_i2"))
        s_gx = es.enter_context(nc.semaphore("s_gx"))
        s_gx2 = es.enter_context(nc.semaphore("s_gx2"))
        s_t2 = es.enter_context(nc.semaphore("s_t2"))
        s_gp = es.enter_context(nc.semaphore("s_gp"))
        s_tr = es.enter_context(nc.semaphore("s_tr"))
        s_pe = es.enter_context(nc.semaphore("s_pe"))
        s_dv = es.enter_context(nc.semaphore("s_dv"))
        s_ac = es.enter_context(nc.semaphore("s_ac"))
        s_st = es.enter_context(nc.semaphore("s_st"))

        class Ctr:
            def __init__(self):
                self.v = 0

            def inc(self, k=1):
                self.v += k
                return self.v

        (c_in, c_i2, c_gx, c_gx2, c_t2, c_gp, c_tr, c_pe, c_dv, c_ac,
         c_st) = (Ctr() for _ in range(11))

        def dma_barrier(eng):
            for sem_, ctr_ in ((s_in, c_in), (s_i2, c_i2), (s_gx, c_gx),
                               (s_gx2, c_gx2), (s_tr, c_tr), (s_t2, c_t2),
                               (s_gp, c_gp), (s_st, c_st)):
                if ctr_.v:
                    eng.wait_ge(sem_, ctr_.v)

        blk = es.enter_context(nc.Block())

        esA = contextlib.ExitStack()

        def sbA(name, shape, dt=F32):
            return esA.enter_context(nc.sbuf_tensor(name, shape, dt))

        with esA:
            ktdup = sbA("ktdup", [128, NB], BF16)
            vaug = sbA("vaug", [128, 65 * NKB], BF16)
            vstg = sbA("vstg", [128, 2 * 64], BF16)
            wqs = sbA("wqs", [128, NDT * 512], BF16)
            wkvs = sbA("wkvs", [128, NDT * 128], BF16)
            wos = sbA("wos", [128, 4 * D], BF16)
            nwb = sbA("nwb_s", [128, D])
            nbb = sbA("nbb_s", [128, D])
            fat = sbA("fat", [16, 6 * 512])
            zrow = sbA("zrow", [65, 4 * 512])
            att = sbA("att", [128, D])
            attjunk = sbA("attjunk", [128, D], BF16)
            xot = sbA("xot", [128, 2 * D])

            esA1 = contextlib.ExitStack()

            def sbA1(name, shape, dt=F32):
                return esA1.enter_context(nc.sbuf_tensor(name, shape, dt))

            with esA1:
                wraw = sbA1("wraw", [128, NDT * 512])
                kvraw = sbA1("kvraw", [128, NB], BF16)
                xsf = sbA1("xsf", [128, 2 * 512])
                xb = sbA1("xb", [128, 4 * 512], BF16)
                x2b = sbA1("x2b", [128, 4 * 512], BF16)
                bcs = sbA1("bcs", [128, 2 * 512])
                bcrow = sbA1("bcrow", [1, 1024], BF16)
                pb12 = esA1.enter_context(nc.psum_tensor("pb12", [128, 1024], F32))

                @blk.gpsimd
                def _(g):
                    for dst, src in ((cwc, cw_h), (cbc, cb_h), (nwc, nw_h),
                                     (nbc, nb_h), (nwb, nwb_h), (nbb, nbb_h),
                                     (hmask, hm_h)):
                        g.dma_start(out=dst[:, :], in_=src[:, :]).then_inc(s_gp, 16)
                        c_gp.inc(16)
                    g.memset(ones_r[:, :], 1.0)
                    g.memset(ones_c[:, :], 1.0)
                    g.memset(halos[:, :], 0.0)
                    g.memset(colb[:, :], 0.0)
                    g.memset(colpad[:, :], 0.0)
                    g.dma_start(out=wraw[:, :], in_=wq_h[:, :]).then_inc(s_gp, 16)
                    c_gp.inc(16)
                P0_GP = c_gp.v

                @blk.vector
                def _(v):
                    v.wait_ge(s_gp, P0_GP)
                    v.tensor_copy(out=wqs[:, :], in_=wraw[:, :]).then_inc(s_dv, 1)
                P0_WQ = c_dv.inc()

                @blk.gpsimd
                def _(g):
                    g.wait_ge(s_dv, P0_WQ)
                    g.dma_start(out=wraw[:, 0:NDT * 128], in_=wkv_h[:, :]).then_inc(s_gp, 16)
                    c_gp.inc(16)
                P0_GP2 = c_gp.v

                @blk.vector
                def _(v):
                    v.wait_ge(s_gp, P0_GP2)
                    v.tensor_copy(out=wkvs[:, :], in_=wraw[:, 0:NDT * 128])
                    v.tensor_copy(out=colbb[:, 0:NDT], in_=nbc[:, :]).then_inc(s_dv, 1)
                P0_KV = c_dv.inc()

                @blk.tensor
                def _(t):
                    t.wait_ge(s_dv, P0_KV)
                    for dt in range(NDT):
                        inst = t.matmul(pb12[0:1, 0:128], colbb[:, dt:dt + 1],
                                        wkvs[:, dt * 128:(dt + 1) * 128],
                                        start=(dt == 0), stop=(dt == NDT - 1))
                    inst.then_inc(s_pe, 1)
                P0_VB = c_pe.inc()

                @blk.vector
                def _(v):
                    v.wait_ge(s_pe, P0_VB)
                    for dt in range(NDT):
                        v.tensor_scalar(out=wkvs[:, dt * 128:(dt + 1) * 128],
                                        in0=wkvs[:, dt * 128:(dt + 1) * 128],
                                        scalar1=nwc[:, dt:dt + 1], scalar2=None,
                                        op0=OP.mult)
                    v.tensor_copy(out=wkvs[:, 0:1], in_=wkvs[:, 0:1]).then_inc(s_dv, 1)
                P0_KVS = c_dv.inc()

                @blk.gpsimd
                def _(g):
                    g.wait_ge(s_dv, P0_KVS)
                    g.dma_start(out=wraw[:, 0:4 * D], in_=wo_h[:, :]).then_inc(s_gp, 16)
                    c_gp.inc(16)
                P0_GP3 = c_gp.v

                @blk.vector
                def _(v):
                    v.wait_ge(s_gp, P0_GP3)
                    v.tensor_copy(out=wos[:, :], in_=wraw[:, 0:4 * D]).then_inc(s_dv, 1)
                P0_DVE = c_dv.inc()

                @blk.tensor
                def _(t):
                    t.wait_ge(s_dv, P0_DVE)
                    for dt in range(NDT):
                        inst = t.matmul(pb12[0:1, 512:640], ones_c[:, :],
                                        wkvs[:, dt * 128:(dt + 1) * 128],
                                        start=(dt == 0), stop=(dt == NDT - 1))
                    inst.then_inc(s_pe, 1)
                P0_UVB = c_pe.inc()

                @blk.vector
                def _(v):
                    v.wait_ge(s_pe, P0_UVB)
                    v.tensor_copy(out=zrow[0:1, 0:128], in_=pb12[0:1, 0:128])
                    v.tensor_copy(out=zrow[0:1, 128:256],
                                  in_=pb12[0:1, 512:640]).then_inc(s_dv, 1)
                P0_UVROW = c_dv.inc()

                @blk.gpsimd
                def _(g):
                    g.wait_ge(s_dv, P0_UVROW)
                    g.dma_start(out=scr_h[0:256], in_=zrow[0:1, 0:256]).then_inc(s_gp, 16)
                    c_gp.inc(16)
                    g.wait_ge(s_gp, c_gp.v)
                    g.dma_start(out=colb[:, CB_VB:CB_VB + 1],
                                in_=scr_h[0:128].rearrange("(p o) -> p o", o=1)
                                ).then_inc(s_gp, 16)
                    c_gp.inc(16)
                    g.dma_start(out=colb[:, CB_U:CB_U + 1],
                                in_=scr_h[128:256].rearrange("(p o) -> p o", o=1)
                                ).then_inc(s_gp, 16)
                    c_gp.inc(16)
                P0_UVCOL = c_gp.v

                @blk.vector
                def _(v):
                    v.wait_ge(s_gp, P0_UVCOL)
                    v.tensor_scalar(out=colb[:, CB_NEGU:CB_NEGU + 1],
                                    in0=colb[:, CB_U:CB_U + 1], scalar1=-1.0,
                                    scalar2=None, op0=OP.mult).then_inc(s_dv, 1)
                P0_NEGU = c_dv.inc()

                P1_LOAD, P1_CAST, P1_MM, P1_EVAC, P1_ROW = {}, {}, {}, {}, {}
                for ch in range(NSC):
                    for dt in range(NDT):
                        li = ch * NDT + dt
                        sx = (li % 2) * 512
                        bx = (li % 4) * 512

                        @blk.sync
                        def _(sy, ch=ch, dt=dt, li=li, sx=sx):
                            if li >= 2:
                                sy.wait_ge(s_dv, P1_CAST[li - 2])
                            sem, ctr = (s_in, c_in) if li % 2 == 0 else (s_i2, c_i2)
                            sy.dma_start(
                                out=xsf[:, sx:sx + 512],
                                in_=xT_h[dt * 128:(dt + 1) * 128,
                                         ch * 512:(ch + 1) * 512]).then_inc(sem, 16)
                            ctr.inc(16)
                            P1_LOAD[li] = ctr.v

                        @blk.vector
                        def _(v, ch=ch, dt=dt, li=li, sx=sx, bx=bx):
                            v.wait_ge(s_in if li % 2 == 0 else s_i2, P1_LOAD[li])
                            if li == 0:
                                v.wait_ge(s_dv, P0_NEGU)
                            if li >= 4:
                                v.wait_ge(s_pe, P1_MM[li - 4])
                            v.tensor_copy(out=xb[:, bx:bx + 512], in_=xsf[:, sx:sx + 512])
                            v.tensor_tensor(out=x2b[:, bx:bx + 512],
                                            in0=xb[:, bx:bx + 512],
                                            in1=xb[:, bx:bx + 512],
                                            op=OP.mult).then_inc(s_dv, 1)
                            c_dv.inc()
                            P1_CAST[li] = c_dv.v

                        @blk.tensor
                        def _(t, ch=ch, dt=dt, li=li, bx=bx):
                            t.wait_ge(s_dv, P1_CAST[li])
                            if dt == 0 and ch >= 1:
                                t.wait_ge(s_dv, P1_EVAC[ch - 1])
                            st_, sp = (dt == 0), (dt == NDT - 1)
                            t.matmul(pb12[0:1, 0:512], ones_c[:, :], xb[:, bx:bx + 512],
                                     start=st_, stop=sp)
                            t.matmul(pb12[0:1, 512:1024], ones_c[:, :],
                                     x2b[:, bx:bx + 512], start=st_, stop=sp)
                            t.matmul(pacc[:, 0:512], wkvs[:, dt * 128:(dt + 1) * 128],
                                     xb[:, bx:bx + 512], start=st_,
                                     stop=sp).then_inc(s_pe, 1)
                            P1_MM[li] = c_pe.inc()

                    @blk.vector
                    def _(v, ch=ch):
                        v.wait_ge(s_pe, P1_MM[ch * NDT + NDT - 1])
                        if ch >= 1:
                            v.wait_ge(s_gp, P1_ROW[ch - 1])
                        v.tensor_copy(out=zrow[0:1, 0:512], in_=pb12[0:1, 0:512])
                        v.tensor_copy(out=zrow[0:1, 512:1024], in_=pb12[0:1, 512:1024])
                        v.tensor_copy(out=kvraw[:, ch * 512:(ch + 1) * 512],
                                      in_=pacc[:, 0:512]).then_inc(s_dv, 1)
                        c_dv.inc()
                        P1_EVAC[ch] = c_dv.v

                    @blk.gpsimd
                    def _(g, ch=ch):
                        g.wait_ge(s_dv, P1_EVAC[ch])
                        g.dma_start(out=scr_h[1024 + ch * 512:1024 + (ch + 1) * 512],
                                    in_=zrow[0:1, 0:512]).then_inc(s_gp, 16)
                        c_gp.inc(16)
                        g.dma_start(out=scr_h[1024 + NB + ch * 512:
                                              1024 + NB + (ch + 1) * 512],
                                    in_=zrow[0:1, 512:1024]).then_inc(s_gp, 16)
                        c_gp.inc(16)
                        P1_ROW[ch] = c_gp.v

                @blk.gpsimd
                def _(g):
                    g.wait_ge(s_gp, P1_ROW[NSC - 1])
                    g.dma_start(out=fat[0:NSC, 0:512],
                                in_=scr_h[1024:1024 + NB].rearrange("(a b) -> a b", a=NSC)
                                ).then_inc(s_gp, 16)
                    c_gp.inc(16)
                    g.dma_start(out=fat[0:NSC, 512:1024],
                                in_=scr_h[1024 + NB:1024 + 2 * NB].rearrange(
                                    "(a b) -> a b", a=NSC)).then_inc(s_gp, 16)
                    c_gp.inc(16)
                P1_FAT = c_gp.v

                SXr = fat[0:NSC, 0:512]
                VARr = fat[0:NSC, 512:1024]
                Ar = fat[0:NSC, 1024:1536]
                Cr = fat[0:NSC, 1536:2048]
                Mr = fat[0:NSC, 2048:2560]
                A1r = fat[0:NSC, 2560:3072]

                @blk.vector
                def _(v):
                    v.wait_ge(s_gp, P1_FAT)
                    v.tensor_scalar(out=Mr, in0=SXr, scalar1=1.0 / D, scalar2=None,
                                    op0=OP.mult)
                    v.scalar_tensor_tensor(out=Ar, in0=Mr, scalar=1.0, op0=OP.mult,
                                           in1=Mr, op1=OP.mult)
                    v.tensor_scalar(out=VARr, in0=VARr, scalar1=1.0 / D, scalar2=None,
                                    op0=OP.mult)
                    v.tensor_tensor(out=VARr, in0=VARr, in1=Ar, op=OP.subtract)
                    v.tensor_scalar(out=Cr, in0=VARr, scalar1=EPS, scalar2=None,
                                    op0=OP.add)
                    v.reciprocal(out=A1r, in_=Cr).then_inc(s_dv, 1)
                P1_R = c_dv.inc()

                @blk.scalar
                def _(s):
                    s.wait_ge(s_dv, P1_R)
                    s.activation(out=A1r, in_=A1r, func=AF.Sqrt, bias=0.0,
                                 scale=1.0).then_inc(s_ac, 1)
                P1_A1 = c_ac.inc()

                @blk.vector
                def _(v):
                    v.wait_ge(s_ac, P1_A1)
                    v.scalar_tensor_tensor(out=Cr, in0=A1r, scalar=1.0, op0=OP.mult,
                                           in1=A1r, op1=OP.mult)
                    v.tensor_tensor(out=Cr, in0=VARr, in1=Cr, op=OP.mult)
                    v.tensor_scalar(out=Cr, in0=Cr, scalar1=EPS, scalar2=None,
                                    op0=OP.add)
                    v.reciprocal(out=Ar, in_=Cr).then_inc(s_dv, 1)
                P1_R2 = c_dv.inc()

                @blk.scalar
                def _(s):
                    s.wait_ge(s_dv, P1_R2)
                    s.activation(out=Ar, in_=Ar, func=AF.Sqrt, bias=0.0,
                                 scale=1.0).then_inc(s_ac, 1)
                P1_A2 = c_ac.inc()

                @blk.vector
                def _(v):
                    v.wait_ge(s_ac, P1_A2)
                    v.tensor_tensor(out=Ar, in0=A1r, in1=Ar, op=OP.mult)
                    v.tensor_tensor(out=Cr, in0=Mr, in1=Ar, op=OP.mult)
                    v.tensor_copy(out=fatb[0:NSC, 0:512], in_=Ar)
                    v.tensor_copy(out=fatb[0:NSC, 512:1024], in_=Cr).then_inc(s_dv, 1)
                P1_AC = c_dv.inc()

                P1C_BC, P1C_KV, P1C_RW = {}, {}, {}
                for ch in range(NSC):
                    @blk.gpsimd
                    def _(g, ch=ch):
                        g.wait_ge(s_dv, P1_AC if ch == 0 else P1C_KV[ch - 1])
                        g.dma_start(out=bcrow[0:1, :],
                                    in_=fatb[ch:ch + 1, 0:1024]).then_inc(s_gp, 16)
                        c_gp.inc(16)
                        P1C_RW[ch] = c_gp.v

                    @blk.tensor
                    def _(t, ch=ch):
                        t.wait_ge(s_gp, P1C_RW[ch])
                        t.matmul(pb12[:, 0:512], ones_r[0:1, :],
                                 bcrow[0:1, 0:512], start=True, stop=True)
                        t.matmul(pb12[:, 512:1024], ones_r[0:1, :],
                                 bcrow[0:1, 512:1024], start=True,
                                 stop=True).then_inc(s_pe, 1)
                        P1C_BC[ch] = c_pe.inc()

                    @blk.vector
                    def _(v, ch=ch):
                        v.wait_ge(s_pe, P1C_BC[ch])
                        cs_ = slice(ch * 512, (ch + 1) * 512)
                        v.tensor_tensor(out=bcs[:, 0:512], in0=kvraw[:, cs_],
                                        in1=pb12[:, 0:512], op=OP.mult)
                        v.scalar_tensor_tensor(out=bcs[:, 512:1024],
                                               in0=pb12[:, 512:1024],
                                               scalar=colb[:, CB_NEGU:CB_NEGU + 1],
                                               op0=OP.mult, in1=bcs[:, 0:512],
                                               op1=OP.add)
                        v.tensor_scalar(out=ktdup[0:64, cs_], in0=bcs[0:64, 512:1024],
                                        scalar1=colb[0:64, CB_VB:CB_VB + 1],
                                        scalar2=None, op0=OP.add)
                        v.tensor_scalar(out=ktdup[64:128, cs_],
                                        in0=bcs[64:128, 512:1024],
                                        scalar1=colb[64:128, CB_VB:CB_VB + 1],
                                        scalar2=None, op0=OP.add).then_inc(s_dv, 1)
                        P1C_KV[ch] = c_dv.inc()
                P1_DONE = c_dv.v

                P1_VTT, P1_VTC = {}, {}
                for j in range(NKB):
                    @blk.sync
                    def _(sy, j=j):
                        if j == 0:
                            sy.wait_ge(s_dv, P1_DONE)
                        if j >= 2:
                            sy.wait_ge(s_dv, P1_VTC[j - 2])
                        sem, ctr = (s_in, c_in) if j % 2 == 0 else (s_i2, c_i2)
                        sy.dma_start_transpose(out=vstg[:, (j % 2) * 64:(j % 2) * 64 + 64],
                                               in_=ktdup[64:128, j * 128:(j + 1) * 128]
                                               ).then_inc(sem, 16)
                        ctr.inc(16)
                        P1_VTT[j] = ctr.v

                    @blk.vector
                    def _(v, j=j):
                        v.wait_ge(s_in if j % 2 == 0 else s_i2, P1_VTT[j])
                        v.tensor_copy(out=vaug[:, j * 65:j * 65 + 64],
                                      in_=vstg[:, (j % 2) * 64:(j % 2) * 64 + 64]
                                      ).then_inc(s_dv, 1)
                        c_dv.inc()
                        P1_VTC[j] = c_dv.v
                P1_VT_DV = c_dv.v

                @blk.vector
                def _(v):
                    v.wait_ge(s_dv, P1_VT_DV)
                    v.tensor_copy(out=ktdup[64:128, :],
                                  in_=ktdup[0:64, :]).then_inc(s_dv, 1)
                P1_KDUP = c_dv.inc()

                @blk.gpsimd
                def _(g):
                    g.wait_ge(s_dv, P1_KDUP)
                    for j in range(NKB):
                        g.memset(vaug[:, j * 65 + 64:j * 65 + 65], 1.0)
                    g.memset(scst[:, :], 0.0)

                @blk.gpsimd
                def _(g):
                    g.dma_start(out=scst[0:1, 0:1],
                                in_=scr_h[0:1].rearrange("(p o) -> p o", o=1)
                                ).then_inc(s_gp, 16)
                    c_gp.inc(16)
                P1_VAUG = c_gp.v

            esA2 = contextlib.ExitStack()

            def sbA2(name, shape, dt=F32):
                return esA2.enter_context(nc.sbuf_tensor(name, shape, dt))

            with esA2:
                qtd = sbA2("qtd", [128, 8 * T], BF16)
                xn2b = sbA2("xn2b", [128, NDT * 512], BF16)
                expb = sbA2("expb", [128, 2 * 2048], BF16)
                rowr = sbA2("rowr", [1, 2048], BF16)
                rz8 = sbA2("rz8", [8, 512])
                rzr = sbA2("rzr", [8, 512])
                rz8b = sbA2("rz8b", [8, 512], BF16)
                rzb = rowr
                rzbc = sbA2("rzbc", [64, 512])
                xqn = sbA2("xqn", [128, 2 * D], BF16)
                xstg = sbA2("xstg", [128, 2 * D], BF16)
                xn2n = sbA2("xn2n", [128, 2 * D], BF16)
                xtmp = sbA2("xtmp", [128, D])
                woin = sbA2("woin", [128, 4 * 512], BF16)
                oddst = sbA2("oddst", [64, 512], BF16)

                esP2 = contextlib.ExitStack()
                pbt = esP2.enter_context(nc.psum_tensor("pbt", [128, 1024], F32))
                zi32 = esP2.enter_context(nc.psum_tensor("zi32p", [128, 1024], I32))
                with esP2:
                    P2_LD, P2_ST, P2_SQ1, P2_CM, P2_SQ2 = {}, {}, {}, {}, {}
                    P2_QX, P2_TRX, P2_TRN, P2_QT, P2_QTD, P2_QTDG = {}, {}, {}, {}, {}, {}
                    P2_TRXC = {}

                    for tt in range(NTT):
                        par = tt % 2
                        t16 = tt % 16

                        @blk.gpsimd
                        def _(g, tt=tt, par=par):
                            if tt == 0:
                                g.wait_ge(s_gp, P1_VAUG)
                                dma_barrier(g)
                            if tt >= 2:
                                g.wait_ge(s_dv, P2_QX[tt - 2])
                            sem, ctr = (s_gx, c_gx) if tt % 2 == 0 else (s_gx2, c_gx2)
                            g.dma_start(out=xot[:, par * D:(par + 1) * D],
                                        in_=xow_h[tt * 128:(tt + 1) * 128, :]
                                        ).then_inc(sem, 16)
                            ctr.inc(16)
                            P2_LD[tt] = ctr.v

                        @blk.vector
                        def _(v, tt=tt, par=par, t16=t16):
                            v.wait_ge(s_gx if tt % 2 == 0 else s_gx2, P2_LD[tt])
                            xo = xot[:, par * D:(par + 1) * D]
                            m1 = colb[:, CB_M1 + t16:CB_M1 + t16 + 1]
                            a1 = colb[:, CB_A1 + t16:CB_A1 + t16 + 1]
                            t0 = colb[:, CB_T0 + t16:CB_T0 + t16 + 1]
                            t1c = colb[:, CB_T1 + t16:CB_T1 + t16 + 1]
                            mx = colb[:, CB_MX + t16:CB_MX + t16 + 1]
                            g_ = colb[:, CB_G + t16:CB_G + t16 + 1]
                            v.reduce_sum(out=t0, in_=xo, axis=AX.X)
                            v.tensor_tensor(out=xtmp[:, :], in0=xo, in1=xo,
                                            op=OP.mult)
                            v.reduce_sum(out=t1c, in_=xtmp[:, :], axis=AX.X)
                            v.reduce_max(out=mx, in_=xo, axis=AX.X)
                            v.tensor_reduce(out=g_, in_=xo, axis=AX.X, op=OP.min)
                            v.tensor_scalar(out=m1, in0=t0, scalar1=1.0 / D,
                                            scalar2=None, op0=OP.mult)
                            v.tensor_scalar(out=t1c, in0=t1c, scalar1=1.0 / D,
                                            scalar2=None, op0=OP.mult)
                            v.scalar_tensor_tensor(out=a1, in0=m1, scalar=1.0,
                                                   op0=OP.mult, in1=m1, op1=OP.mult)
                            v.tensor_tensor(out=mx, in0=mx, in1=m1, op=OP.subtract)
                            v.tensor_tensor(out=g_, in0=m1, in1=g_, op=OP.subtract)
                            v.tensor_tensor(out=a1, in0=t1c, in1=a1, op=OP.subtract)
                            v.tensor_tensor(out=mx, in0=mx, in1=g_, op=OP.max)
                            v.tensor_scalar(out=a1, in0=a1, scalar1=EPS, scalar2=None,
                                            op0=OP.add)
                            v.tensor_copy(out=t0, in_=t1c)
                            v.reciprocal(out=a1, in_=a1).then_inc(s_dv, 1)
                            c_dv.inc()
                            P2_ST[tt] = c_dv.v

                        @blk.scalar
                        def _(s, tt=tt, t16=t16):
                            s.wait_ge(s_dv, P2_ST[tt])
                            a1 = colb[:, CB_A1 + t16:CB_A1 + t16 + 1]
                            s.activation(out=a1, in_=a1, func=AF.Sqrt, bias=0.0,
                                         scale=1.0).then_inc(s_ac, 1)
                            c_ac.inc()
                            P2_SQ1[tt] = c_ac.v

                        @blk.vector
                        def _(v, tt=tt, t16=t16):
                            v.wait_ge(s_ac, P2_SQ1[tt])
                            a1 = colb[:, CB_A1 + t16:CB_A1 + t16 + 1]
                            t0 = colb[:, CB_T0 + t16:CB_T0 + t16 + 1]
                            t1c = colb[:, CB_T1 + t16:CB_T1 + t16 + 1]
                            g_ = colb[:, CB_G + t16:CB_G + t16 + 1]
                            mx = colb[:, CB_MX + t16:CB_MX + t16 + 1]
                            iq = colb[:, CB_IQ + t16:CB_IQ + t16 + 1]
                            v.scalar_tensor_tensor(out=t0, in0=a1, scalar=1.0,
                                                   op0=OP.mult, in1=a1, op1=OP.mult)
                            v.tensor_tensor(out=mx, in0=mx, in1=a1, op=OP.mult)
                            v.reciprocal(out=t1c, in_=t0)
                            v.tensor_scalar(out=mx, in0=mx, scalar1=1e-5,
                                            scalar2=None, op0=OP.max)
                            v.tensor_scalar(out=t1c, in0=t1c, scalar1=-EPS,
                                            scalar2=None, op0=OP.add)
                            v.tensor_scalar(out=iq, in0=mx, scalar1=1.0 / 127.0,
                                            scalar2=None, op0=OP.mult)
                            v.tensor_tensor(out=t0, in0=t1c, in1=t0, op=OP.mult)
                            v.reciprocal(out=g_, in_=mx)
                            v.tensor_scalar(out=t0, in0=t0, scalar1=EPS, scalar2=None,
                                            op0=OP.add)
                            v.tensor_scalar(out=g_, in0=g_, scalar1=127.0,
                                            scalar2=None, op0=OP.mult)
                            v.reciprocal(out=t0, in_=t0)
                            v.tensor_tensor(out=g_, in0=a1, in1=g_,
                                            op=OP.mult).then_inc(s_dv, 1)
                            c_dv.inc()
                            P2_CM[tt] = c_dv.v

                        @blk.scalar
                        def _(s, tt=tt, t16=t16):
                            s.wait_ge(s_dv, P2_CM[tt])
                            t0 = colb[:, CB_T0 + t16:CB_T0 + t16 + 1]
                            s.activation(out=t0, in_=t0, func=AF.Sqrt, bias=0.0,
                                         scale=1.0).then_inc(s_ac, 1)
                            c_ac.inc()
                            P2_SQ2[tt] = c_ac.v

                        @blk.vector
                        def _(v, tt=tt, par=par, t16=t16):
                            v.wait_ge(s_ac, P2_SQ2[tt])
                            if tt >= 2:
                                v.wait_ge(s_tr if tt % 2 == 0 else s_t2,
                                          P2_TRN[tt - 2])
                                v.wait_ge(s_in if tt % 2 == 0 else s_i2,
                                          P2_TRX[tt - 2])
                            xo = xot[:, par * D:(par + 1) * D]
                            m1 = colb[:, CB_M1 + t16:CB_M1 + t16 + 1]
                            a1 = colb[:, CB_A1 + t16:CB_A1 + t16 + 1]
                            aa = colb[:, CB_A + t16:CB_A + t16 + 1]
                            g_ = colb[:, CB_G + t16:CB_G + t16 + 1]
                            t0 = colb[:, CB_T0 + t16:CB_T0 + t16 + 1]
                            v.tensor_tensor(out=aa, in0=a1, in1=t0, op=OP.mult)
                            v.tensor_scalar(out=xtmp[:, :], in0=xo, scalar1=m1,
                                            scalar2=g_, op0=OP.subtract, op1=OP.mult)
                            v.tensor_scalar(out=xtmp[:, :], in0=xtmp[:, :],
                                            scalar1=128.5, scalar2=None, op0=OP.add)
                            v.tensor_copy(out=zi32[:, :], in_=xtmp[:, :])
                            v.tensor_copy(out=woin[:, 0:D], in_=zi32[:, :])
                            v.tensor_tensor(out=attjunk[:, :], in0=woin[:, 0:D],
                                            in1=xtmp[:, :], op=OP.is_le)
                            v.tensor_tensor(out=woin[:, 0:D], in0=woin[:, 0:D],
                                            in1=attjunk[:, :], op=OP.add)
                            v.tensor_scalar(out=xqn[:, par * D:(par + 1) * D],
                                            in0=woin[:, 0:D], scalar1=-129.0,
                                            scalar2=127.0, op0=OP.add, op1=OP.min)
                            v.tensor_scalar(out=xtmp[:, :], in0=xo, scalar1=m1,
                                            scalar2=aa, op0=OP.subtract, op1=OP.mult)
                            v.tensor_tensor(out=xtmp[:, :], in0=xtmp[:, :],
                                            in1=nwb[:, :], op=OP.mult)
                            v.tensor_tensor(out=xn2n[:, par * D:(par + 1) * D],
                                            in0=xtmp[:, :], in1=nbb[:, :],
                                            op=OP.add).then_inc(s_dv, 1)
                            c_dv.inc()
                            P2_QX[tt] = c_dv.v

                        @blk.scalar
                        def _(s, tt=tt, par=par):
                            s.wait_ge(s_dv, P2_QX[tt])
                            if tt >= 2:
                                s.wait_ge(s_dv, P2_TRXC[tt - 2])
                            if tt // 4 >= 1 and tt % 4 == 0:
                                s.wait_ge(s_pe, P2_QT[tt // 4 - 1])
                            semx, ctrx = (s_in, c_in) if tt % 2 == 0 else \
                                (s_i2, c_i2)
                            for dt in range(NDT):
                                s.dma_start_transpose(
                                    out=xstg[:, par * D + dt * 128:
                                             par * D + (dt + 1) * 128],
                                    in_=xqn[:, par * D + dt * 128:
                                            par * D + (dt + 1) * 128]
                                    ).then_inc(semx, 16)
                                ctrx.inc(16)
                            P2_TRX[tt] = ctrx.v
                            sem, ctr = (s_tr, c_tr) if tt % 2 == 0 else (s_t2, c_t2)
                            for dt in range(NDT):
                                s.dma_start_transpose(
                                    out=xn2b[:, dt * 512 + (tt % 4) * 128:
                                             dt * 512 + (tt % 4 + 1) * 128],
                                    in_=xn2n[:, par * D + dt * 128:
                                             par * D + (dt + 1) * 128]
                                    ).then_inc(sem, 16)
                                ctr.inc(16)
                            P2_TRN[tt] = ctr.v

                        @blk.vector
                        def _(v, tt=tt, par=par):
                            v.wait_ge(s_in if tt % 2 == 0 else s_i2, P2_TRX[tt])
                            v.tensor_copy(
                                out=xnq[:, :].rearrange("p (n t) -> p n t", n=NDT)[
                                    :, :, 1 + tt * 128:1 + (tt + 1) * 128],
                                in_=xstg[:, par * D:(par + 1) * D].rearrange(
                                    "p (n t) -> p n t", n=NDT)).then_inc(s_dv, 1)
                            c_dv.inc()
                            P2_TRXC[tt] = c_dv.v

                        if tt % 4 == 3:
                            qc = tt // 4
                            for qt in range(4):
                                hev, hod = 2 * qt, 2 * qt + 1

                                @blk.tensor
                                def _(t, qc=qc, qt=qt, tt=tt):
                                    if qt == 0:
                                        t.wait_ge(s_tr, P2_TRN[tt - 1])
                                        t.wait_ge(s_t2, P2_TRN[tt])
                                    if qt >= 1:
                                        t.wait_ge(s_dv, P2_QTD[(qc, qt - 1)])
                                    elif qc >= 1:
                                        t.wait_ge(s_dv, P2_QTD[(qc - 1, 3)])
                                    for dt in range(NDT):
                                        inst = t.matmul(
                                            pacc[:, 512:1024],
                                            wqs[:, dt * 512 + qt * 128:
                                                dt * 512 + (qt + 1) * 128],
                                            xn2b[:, dt * 512:(dt + 1) * 512],
                                            start=(dt == 0), stop=(dt == NDT - 1))
                                    inst.then_inc(s_pe, 1)
                                    c_pe.inc()
                                    P2_QT[(qc, qt)] = c_pe.v
                                    if qt == 3:
                                        P2_QT[qc] = c_pe.v

                                @blk.vector
                                def _(v, qc=qc, hev=hev, hod=hod, qt=qt):
                                    v.wait_ge(s_pe, P2_QT[(qc, qt)])
                                    v.tensor_copy(
                                        out=qtd[0:64, hev * T + qc * 512:
                                                hev * T + qc * 512 + 512],
                                        in_=pacc[0:64, 512:1024])
                                    v.tensor_copy(
                                        out=qtd[64:128, hod * T + qc * 512:
                                                hod * T + qc * 512 + 512],
                                        in_=pacc[64:128, 512:1024]).then_inc(s_dv, 1)
                                    c_dv.inc()
                                    P2_QTD[(qc, qt)] = c_dv.v

                                @blk.gpsimd
                                def _(g, qc=qc, hev=hev, hod=hod, qt=qt):
                                    g.wait_ge(s_dv, P2_QTD[(qc, qt)])
                                    g.dma_start(
                                        out=qtd[64:128, hev * T + qc * 512:
                                                hev * T + qc * 512 + 512],
                                        in_=qtd[0:64, hev * T + qc * 512:
                                                hev * T + qc * 512 + 512]
                                        ).then_inc(s_gp, 16)
                                    c_gp.inc(16)
                                    g.dma_start(
                                        out=qtd[0:64, hod * T + qc * 512:
                                                hod * T + qc * 512 + 512],
                                        in_=qtd[64:128, hod * T + qc * 512:
                                                hod * T + qc * 512 + 512]
                                        ).then_inc(s_gp, 16)
                                    c_gp.inc(16)
                                    P2_QTDG[(qc, qt)] = c_gp.v
                    P2_TILE_DONE = c_dv.v
                    P2_QTD_GP = c_gp.v

                    @blk.vector
                    def _(v):
                        v.wait_ge(s_dv, P2_TILE_DONE)
                        v.tensor_copy(out=colpad[:, 0:16],
                                      in_=colb[:, CB_IQ:CB_IQ + 16]).then_inc(s_dv, 1)
                    P2_IQB = c_dv.inc()

                    @blk.scalar
                    def _(s):
                        s.wait_ge(s_dv, P2_IQB)
                        s.dma_start_transpose(out=rowt[:, :],
                                              in_=colpad[:, :]).then_inc(s_tr, 16)
                        c_tr.inc(16)
                    P2_IQT = c_tr.v

                    @blk.gpsimd
                    def _(g):
                        g.wait_ge(s_tr, P2_IQT)
                        g.dma_start(out=rowr[0:1, 0:NTT * 128],
                                    in_=rowt[0:NTT, 0:128]).then_inc(s_gp, 16)
                        c_gp.inc(16)
                    P2_IQR = c_gp.v

                    P2_IQM, P2_IQE = {}, {}
                    for qc in range(NQC):
                        @blk.tensor
                        def _(t, qc=qc):
                            if qc == 0:
                                t.wait_ge(s_gp, P2_IQR)
                            else:
                                t.wait_ge(s_dv, P2_IQE[qc - 1])
                            t.matmul(pbt[:, 0:512], ones_r[0:1, :],
                                     rowr[0:1, qc * 512:(qc + 1) * 512], start=True,
                                     stop=True).then_inc(s_pe, 1)
                            P2_IQM[qc] = c_pe.inc()

                        @blk.vector
                        def _(v, qc=qc):
                            v.wait_ge(s_pe, P2_IQM[qc])
                            v.tensor_copy(out=invqsb[:, 1 + qc * 512:1 + (qc + 1) * 512],
                                          in_=pbt[:, 0:512]).then_inc(s_dv, 1)
                            c_dv.inc()
                            P2_IQE[qc] = c_dv.v
                    P2_IQDONE = c_dv.v

                    HL = {}
                    for sd in range(2):
                        frow, fcol = (NSC - 1, 511) if sd == 0 else (T // 512, 0)
                        xcol = NB - 1 if sd == 0 else T

                        @blk.gpsimd
                        def _(g, sd=sd, frow=frow, fcol=fcol, xcol=xcol):
                            g.wait_ge(s_dv, P2_IQDONE)
                            g.wait_ge(s_gp, c_gp.v)
                            if sd == 1:
                                g.wait_ge(s_dv, HL[(0, "rw")])
                            g.dma_start(out=scst[0:1, 8 * sd:8 * sd + 1],
                                        in_=fat[frow:frow + 1,
                                                2048 + fcol:2048 + fcol + 1]
                                        ).then_inc(s_gp, 16)
                            c_gp.inc(16)
                            g.dma_start(out=scst[0:1, 8 * sd + 1:8 * sd + 2],
                                        in_=fat[frow:frow + 1,
                                                2560 + fcol:2560 + fcol + 1]
                                        ).then_inc(s_gp, 16)
                            c_gp.inc(16)
                            for dt in range(NDT):
                                g.dma_start(out=halos[:, 16 * sd + dt:16 * sd + dt + 1],
                                            in_=xT_h[dt * 128:(dt + 1) * 128,
                                                     xcol:xcol + 1],
                                            allow_slow_non_contiguous=True
                                            ).then_inc(s_gp, 16)
                                c_gp.inc(16)
                            HL[(sd, "sc")] = c_gp.v

                        @blk.vector
                        def _(v, sd=sd):
                            v.wait_ge(s_gp, HL[(sd, "sc")])
                            v.tensor_copy(out=scstb[0:1, 4 * sd:4 * sd + 2],
                                          in_=scst[0:1, 8 * sd:8 * sd + 2]
                                          ).then_inc(s_dv, 1)
                            c_dv.inc()
                            HL[(sd, "cb")] = c_dv.v

                        @blk.tensor
                        def _(t, sd=sd):
                            t.wait_ge(s_dv, HL[(sd, "cb")])
                            t.matmul(pbt[:, 0:1], ones_r[0:1, :],
                                     scstb[0:1, 4 * sd:4 * sd + 1], start=True,
                                     stop=True)
                            t.matmul(pbt[:, 1:2], ones_r[0:1, :],
                                     scstb[0:1, 4 * sd + 1:4 * sd + 2], start=True,
                                     stop=True).then_inc(s_pe, 1)
                            HL[(sd, "rep")] = c_pe.inc()

                        @blk.vector
                        def _(v, sd=sd):
                            v.wait_ge(s_pe, HL[(sd, "rep")])
                            v.tensor_copy(out=halos[:, 32 + 4 * sd:32 + 4 * sd + 1],
                                          in_=pbt[:, 0:1])
                            v.tensor_copy(out=halos[:, 33 + 4 * sd:33 + 4 * sd + 1],
                                          in_=pbt[:, 1:2])
                            v.tensor_copy(out=colpad[:, 2:3],
                                          in_=colb[:, CB_M1:CB_M1 + 1])
                            v.tensor_scalar(out=halos[:, 16 * sd:16 * sd + NDT],
                                            in0=halos[:, 16 * sd:16 * sd + NDT],
                                            scalar1=halos[:, 32 + 4 * sd:
                                                          32 + 4 * sd + 1],
                                            scalar2=halos[:, 33 + 4 * sd:
                                                          33 + 4 * sd + 1],
                                            op0=OP.subtract, op1=OP.mult)
                            v.tensor_copy(out=colpad[:, 2:3],
                                          in_=colb[:, CB_M1:CB_M1 + 1])
                            v.reduce_max(out=halos[:, 34 + 4 * sd:34 + 4 * sd + 1],
                                         in_=halos[:, 16 * sd:16 * sd + NDT],
                                         axis=AX.X,
                                         apply_absolute_value=True).then_inc(s_dv, 1)
                            c_dv.inc()
                            HL[(sd, "xn")] = c_dv.v

                        @blk.gpsimd
                        def _(g, sd=sd):
                            g.wait_ge(s_dv, HL[(sd, "xn")])
                            g.dma_start(out=scr_h[512:640],
                                        in_=halos[:, 34 + 4 * sd:34 + 4 * sd + 1]
                                        ).then_inc(s_gp, 16)
                            c_gp.inc(16)
                            g.wait_ge(s_gp, c_gp.v)
                            g.dma_start(out=zrow[0:1, 1024:1152],
                                        in_=scr_h[512:640]).then_inc(s_gp, 16)
                            c_gp.inc(16)
                            HL[(sd, "bn")] = c_gp.v

                        @blk.vector
                        def _(v, sd=sd):
                            v.wait_ge(s_gp, HL[(sd, "bn")])
                            v.reduce_max(out=scst[0:1, 48 + sd:48 + sd + 1],
                                         in_=zrow[0:1, 1024:1152], axis=AX.X)
                            v.tensor_copy(out=colpad[:, 2:3],
                                          in_=colb[:, CB_M1:CB_M1 + 1])
                            v.tensor_scalar(out=scst[0:1, 48 + sd:48 + sd + 1],
                                            in0=scst[0:1, 48 + sd:48 + sd + 1],
                                            scalar1=1e-5, scalar2=None, op0=OP.max)
                            v.tensor_copy(out=colpad[:, 2:3],
                                          in_=colb[:, CB_M1:CB_M1 + 1])
                            v.reciprocal(out=scst[0:1, 50 + sd:50 + sd + 1],
                                         in_=scst[0:1, 48 + sd:48 + sd + 1])
                            v.tensor_scalar(out=scst[0:1, 52 + sd:52 + sd + 1],
                                            in0=scst[0:1, 48 + sd:48 + sd + 1],
                                            scalar1=1.0 / 127.0, scalar2=None,
                                            op0=OP.mult)
                            v.tensor_scalar(out=scst[0:1, 50 + sd:50 + sd + 1],
                                            in0=scst[0:1, 50 + sd:50 + sd + 1],
                                            scalar1=127.0, scalar2=None, op0=OP.mult)
                            v.tensor_copy(out=scstb[0:1, 9 + 2 * sd:9 + 2 * sd + 1],
                                          in_=scst[0:1, 52 + sd:52 + sd + 1])
                            v.tensor_copy(out=scstb[0:1, 8 + 2 * sd:8 + 2 * sd + 1],
                                          in_=scst[0:1, 50 + sd:50 + sd + 1]
                                          ).then_inc(s_dv, 1)
                            c_dv.inc()
                            HL[(sd, "rw")] = c_dv.v

                        @blk.tensor
                        def _(t, sd=sd):
                            t.wait_ge(s_dv, HL[(sd, "rw")])
                            t.matmul(pbt[:, 2:3], ones_r[0:1, :],
                                     scstb[0:1, 8 + 2 * sd:8 + 2 * sd + 1],
                                     start=True, stop=True)
                            t.matmul(pbt[:, 3:4], ones_r[0:1, :],
                                     scstb[0:1, 9 + 2 * sd:9 + 2 * sd + 1],
                                     start=True, stop=True).then_inc(s_pe, 1)
                            HL[(sd, "q")] = c_pe.inc()

                        @blk.vector
                        def _(v, sd=sd):
                            v.wait_ge(s_pe, HL[(sd, "q")])
                            xq_col = 0 if sd == 0 else T + 1
                            q_sl = halos[:, 16 * sd:16 * sd + NDT]
                            f_sl = halos[:, 32:32 + NDT]
                            t_sl = halos[:, 24:24 + NDT]

                            def spc():
                                v.tensor_copy(out=colpad[:, 2:3],
                                              in_=colb[:, CB_M1:CB_M1 + 1])
                            v.tensor_scalar(out=q_sl, in0=q_sl,
                                            scalar1=pbt[:, 2:3], scalar2=128.5,
                                            op0=OP.mult, op1=OP.add)
                            v.tensor_copy(out=invqsb[:, xq_col:xq_col + 1],
                                          in_=pbt[:, 3:4])
                            v.tensor_copy(out=zi32[:, 0:NDT], in_=q_sl)
                            spc()
                            v.tensor_copy(out=f_sl, in_=zi32[:, 0:NDT])
                            spc()
                            v.tensor_tensor(out=t_sl, in0=f_sl, in1=q_sl,
                                            op=OP.is_le)
                            spc()
                            v.tensor_tensor(out=f_sl, in0=f_sl, in1=t_sl,
                                            op=OP.add)
                            spc()
                            v.tensor_scalar(out=q_sl, in0=f_sl, scalar1=-129.0,
                                            scalar2=127.0, op0=OP.add, op1=OP.min)
                            spc()
                            v.tensor_scalar(out=q_sl, in0=q_sl,
                                            scalar1=hmask[:, sd:sd + 1],
                                            scalar2=None, op0=OP.mult)
                            spc()
                            v.tensor_copy(
                                out=xnq[:, :].rearrange("p (n t) -> p n t",
                                                        n=NDT)[:, :, xq_col],
                                in_=q_sl).then_inc(s_dv, 1)
                            c_dv.inc()
                            HL[(sd, "done")] = c_dv.v
                    P2_DONE = c_dv.v

                pb3 = esA2.enter_context(nc.psum_tensor("pb3", [128, 2048], F32))
                P3_SIM, P3_EXP, P3_AVD = {}, {}, {}
                P3_ZE, P3_RZ, P3_RZB, P3_PRZ, P3_WOIN, P3_ODD = {}, {}, {}, {}, {}, {}
                P3_WO, P3_EV1, P3_EV2, P3_SP = {}, {}, {}, {}

                sets = [(qc, st_) for qc in range(NQC) for st_ in range(2)]
                for si, (qc, st_) in enumerate(sets):
                    prev = sets[si - 1] if si > 0 else None
                    for kb in range(NKB):
                        for hf in range(2):
                            @blk.tensor
                            def _(t, qc=qc, st_=st_, kb=kb, hf=hf, prev=prev,
                                  si=si):
                                if si == 0 and kb == 0 and hf == 0:
                                    t.wait_ge(s_dv, P2_DONE)
                                    t.wait_ge(s_gp, P2_QTD_GP)
                                if kb == 0 and hf == 0 and prev is not None:
                                    t.wait_ge(s_dv, P3_WOIN[(prev[0], prev[1], 3)])
                                    t.wait_ge(s_gp, P3_ODD[(prev[0], prev[1], 3)])
                                    if st_ == 0:
                                        t.wait_ge(s_dv, P3_EV1[(qc - 1, 3)])
                                if kb > 0:
                                    t.wait_ge(s_ac,
                                              P3_EXP[(qc, st_, kb - 1, hf)])
                                    for j in (2 * hf, 2 * hf + 1):
                                        t.matmul(
                                            pacc[0:65, j * 512:j * 512 + 512],
                                            vaug[:, (kb - 1) * 65:
                                                 (kb - 1) * 65 + 65],
                                            expb[:, ((kb - 1) % 2) * 2048 +
                                                 j * 512:
                                                 ((kb - 1) % 2) * 2048 +
                                                 (j + 1) * 512],
                                            start=(kb - 1 == 0),
                                            stop=(kb - 1 == NKB - 1))
                                for j in (2 * hf, 2 * hf + 1):
                                    h = 4 * st_ + j
                                    inst = t.matmul(
                                        pb3[:, j * 512:(j + 1) * 512],
                                        ktdup[:, kb * 128:(kb + 1) * 128],
                                        qtd[:, h * T + qc * 512:
                                            h * T + (qc + 1) * 512],
                                        start=True, stop=True)
                                inst.then_inc(s_pe, 1)
                                P3_SIM[(qc, st_, kb, hf)] = c_pe.inc()

                            @blk.scalar
                            def _(s, qc=qc, st_=st_, kb=kb, hf=hf):
                                s.wait_ge(s_pe, P3_SIM[(qc, st_, kb, hf)])
                                s.activation(
                                    out=expb[:, (kb % 2) * 2048 + hf * 1024:
                                             (kb % 2) * 2048 + hf * 1024 + 1024],
                                    in_=pb3[:, hf * 1024:hf * 1024 + 1024],
                                    func=AF.Exp, bias=0.0,
                                    scale=0.0625).then_inc(s_ac, 1)
                                c_ac.inc()
                                P3_EXP[(qc, st_, kb, hf)] = c_ac.v

                    @blk.tensor
                    def _(t, qc=qc, st_=st_):
                        t.wait_ge(s_ac, P3_EXP[(qc, st_, NKB - 1, 1)])
                        for j in range(4):
                            inst = t.matmul(
                                pacc[0:65, j * 512:j * 512 + 512],
                                vaug[:, (NKB - 1) * 65:(NKB - 1) * 65 + 65],
                                expb[:, ((NKB - 1) % 2) * 2048 + j * 512:
                                     ((NKB - 1) % 2) * 2048 + (j + 1) * 512],
                                start=(NKB - 1 == 0), stop=True)
                        inst.then_inc(s_pe, 1)
                        P3_AVD[(qc, st_)] = c_pe.inc()

                    @blk.vector
                    def _(v, qc=qc, st_=st_):
                        v.wait_ge(s_pe, P3_AVD[(qc, st_)])
                        for j in range(4):
                            v.tensor_copy(out=zrow[64:65, j * 512:(j + 1) * 512],
                                          in_=pacc[64:65, j * 512:(j + 1) * 512])
                        v.tensor_copy(out=zrow[64:65, 0:1],
                                      in_=zrow[64:65, 0:1]).then_inc(s_dv, 1)
                        c_dv.inc()
                        P3_ZE[(qc, st_)] = c_dv.v

                    @blk.gpsimd
                    def _(g, qc=qc, st_=st_):
                        g.wait_ge(s_dv, P3_ZE[(qc, st_)])
                        g.wait_ge(s_gp, c_gp.v)
                        g.dma_start(out=rz8[0:4, :],
                                    in_=zrow[64:65, 0:2048]).then_inc(s_gp, 16)
                        c_gp.inc(16)
                        P3_RZ[(qc, st_)] = c_gp.v

                    @blk.vector
                    def _(v, qc=qc, st_=st_):
                        v.wait_ge(s_gp, P3_RZ[(qc, st_)])
                        v.reciprocal(out=rzr[0:4, :], in_=rz8[0:4, :])
                        v.tensor_copy(out=rz8b[0:4, :],
                                      in_=rzr[0:4, :]).then_inc(s_dv, 1)
                        c_dv.inc()
                        P3_RZB[(qc, st_)] = c_dv.v

                    @blk.gpsimd
                    def _(g, qc=qc, st_=st_):
                        g.wait_ge(s_dv, P3_RZB[(qc, st_)])
                        g.wait_ge(s_gp, c_gp.v)
                        g.dma_start(out=rzb[0:1, 0:2048],
                                    in_=rz8b[0:4, 0:512]).then_inc(s_gp, 16)
                        c_gp.inc(16)
                        P3_RZ[(qc, st_, 2)] = c_gp.v

                    for j in range(4):
                        h = 4 * st_ + j
                        it = h // 2

                        @blk.tensor
                        def _(t, qc=qc, st_=st_, j=j):
                            if j == 0:
                                t.wait_ge(s_gp, P3_RZ[(qc, st_, 2)])
                            t.matmul(pb3[0:64, j * 512:(j + 1) * 512],
                                     ones_r[0:1, 0:64],
                                     rzb[0:1, j * 512:(j + 1) * 512],
                                     start=True, stop=True).then_inc(s_pe, 1)
                            P3_PRZ[(qc, st_, j)] = c_pe.inc()

                        @blk.vector
                        def _(v, qc=qc, st_=st_, j=j, h=h, it=it):
                            v.wait_ge(s_pe, P3_PRZ[(qc, st_, j)])
                            if h % 2 == 1 and j == 3:
                                v.wait_ge(s_gp, P3_ODD[(qc, st_, 1)])
                            v.tensor_copy(out=rzbc[0:64, :],
                                          in_=pb3[0:64, j * 512:(j + 1) * 512])
                            if h % 2 == 0:
                                v.tensor_tensor(
                                    out=woin[0:64, it * 512:(it + 1) * 512],
                                    in0=pacc[0:64, j * 512:(j + 1) * 512],
                                    in1=rzbc[0:64, :],
                                    op=OP.mult).then_inc(s_dv, 1)
                            else:
                                v.tensor_tensor(
                                    out=oddst[0:64, 0:512],
                                    in0=pacc[0:64, j * 512:(j + 1) * 512],
                                    in1=rzbc[0:64, :],
                                    op=OP.mult).then_inc(s_dv, 1)
                            c_dv.inc()
                            P3_WOIN[(qc, st_, j)] = c_dv.v

                        if h % 2 == 1:
                            @blk.gpsimd
                            def _(g, qc=qc, st_=st_, j=j, it=it):
                                g.wait_ge(s_dv, P3_WOIN[(qc, st_, j)])
                                g.wait_ge(s_gp, c_gp.v)
                                g.dma_start(out=woin[64:128, it * 512:(it + 1) * 512],
                                            in_=oddst[0:64, 0:512]).then_inc(s_gp, 16)
                                c_gp.inc(16)
                                P3_ODD[(qc, st_, j)] = c_gp.v

                    if st_ == 1:
                        for isub in range(4):
                            tix = qc * 4 + isub
                            base = (isub % 2) * 1024

                            @blk.tensor
                            def _(t, qc=qc, isub=isub, base=base):
                                if isub == 0:
                                    t.wait_ge(s_dv, P3_WOIN[(qc, 1, 3)])
                                    t.wait_ge(s_gp, P3_ODD[(qc, 1, 3)])
                                if isub >= 2:
                                    t.wait_ge(s_dv, P3_EV1[(qc, isub - 2)])
                                for ec in range(2):
                                    for it in range(4):
                                        inst = t.matmul(
                                            pb3[:, base + ec * 512:
                                                base + (ec + 1) * 512],
                                            woin[:, it * 512 + isub * 128:
                                                 it * 512 + (isub + 1) * 128],
                                            wos[:, it * D + ec * 512:
                                                it * D + (ec + 1) * 512],
                                            start=(it == 0), stop=(it == 3))
                                inst.then_inc(s_pe, 1)
                                c_pe.inc()
                                P3_WO[(qc, isub)] = c_pe.v

                            @blk.vector
                            def _(v, qc=qc, isub=isub, tix=tix, base=base):
                                v.wait_ge(s_pe, P3_WO[(qc, isub)])
                                if tix >= 1:
                                    v.wait_ge(s_in, P3_SP[tix - 1])
                                v.tensor_copy(out=att[:, :],
                                              in_=pb3[:, base:base + 1024])
                                v.reduce_sum(out=colb[:, CB_S1 + tix:
                                                       CB_S1 + tix + 1],
                                             in_=att[:, :], axis=AX.X)
                                v.scalar_tensor_tensor(
                                    out=attjunk[:, :], in0=att[:, :], scalar=1.0,
                                    op0=OP.mult, in1=att[:, :], op1=OP.mult,
                                    accum_out=colb[:, CB_S2 + tix:CB_S2 + tix + 1]
                                    ).then_inc(s_dv, 1)
                                c_dv.inc()
                                P3_EV1[(qc, isub)] = c_dv.v
                                P3_EV2[tix] = c_dv.v

                            @blk.sync
                            def _(sy, tix=tix):
                                sy.wait_ge(s_dv, P3_EV2[tix])
                                sy.dma_start(out=spill_h[tix * 128:(tix + 1) * 128, :],
                                             in_=att[:, :]).then_inc(s_in, 16)
                                c_in.inc(16)
                                P3_SP[tix] = c_in.v
                P3_DONE_DV = c_dv.v
                P3_DONE_AC = c_ac.v
                P3_SP_LAST = c_in.v
                if dbg:
                    @blk.sync
                    def _(sy):
                        sy.wait_ge(s_dv, P3_DONE_DV)
                        sy.wait_ge(s_ac, P3_DONE_AC)
                        sy.dma_start(out=dbg_cb, in_=colb[:, :]).then_inc(s_st, 16)
                        c_st.inc(16)
                        for dst, srcb in ((dbg_xnq, xnq[:, :]), (dbg_qtd, qtd[:, :]),
                                          (dbg_kt, ktdup[:, :]), (dbg_va, vaug[:, :]),
                                          (dbg_iq, invqsb[:, :]),
                                          (dbg_exp, expb[:, 0:2048]),
                                          (dbg_xn2b, xn2b[:, :])):
                            sy.dma_start(out=dst, in_=srcb).then_inc(s_st, 16)
                            c_st.inc(16)

        NT16 = 4 * NQC

        @blk.vector
        def _(v):
            v.wait_ge(s_dv, P3_DONE_DV)
            v.wait_ge(s_ac, P3_DONE_AC)
            s1 = colb[:, CB_S1:CB_S1 + NT16]
            s2 = colb[:, CB_S2:CB_S2 + NT16]
            la = colb[:, CB_LNA:CB_LNA + NT16]
            jk = colb[:, CB_T0:CB_T0 + NT16]
            v.tensor_scalar(out=s1, in0=s1, scalar1=1.0 / D, scalar2=None, op0=OP.mult)
            v.tensor_scalar(out=s2, in0=s2, scalar1=1.0 / D, scalar2=None, op0=OP.mult)
            v.scalar_tensor_tensor(out=la, in0=s1, scalar=1.0, op0=OP.mult,
                                   in1=s1, op1=OP.mult)
            v.tensor_copy(out=jk, in_=s1)
            v.tensor_tensor(out=s2, in0=s2, in1=la, op=OP.subtract)
            v.tensor_copy(out=jk, in_=s1)
            v.tensor_scalar(out=s2, in0=s2, scalar1=EPS, scalar2=None, op0=OP.add)
            v.tensor_copy(out=jk, in_=s1)
            v.reciprocal(out=la, in_=s2).then_inc(s_dv, 1)
        P35_R = c_dv.inc()

        @blk.scalar
        def _(s):
            s.wait_ge(s_dv, P35_R)
            la = colb[:, CB_LNA:CB_LNA + NT16]
            s.activation(out=la, in_=la, func=AF.Sqrt, bias=0.0,
                         scale=1.0).then_inc(s_ac, 1)
        P35_A = c_ac.inc()

        @blk.vector
        def _(v):
            v.wait_ge(s_ac, P35_A)
            s1 = colb[:, CB_S1:CB_S1 + NT16]
            la = colb[:, CB_LNA:CB_LNA + NT16]
            lb = colb[:, CB_LNB:CB_LNB + NT16]
            v.scalar_tensor_tensor(out=lb, in0=s1, scalar=-1.0, op0=OP.mult,
                                   in1=la, op1=OP.mult).then_inc(s_dv, 1)
        P35_DONE = c_dv.inc()

        esB = contextlib.ExitStack()

        def sbB(name, shape, dt=F32):
            return esB.enter_context(nc.sbuf_tensor(name, shape, dt))

        with esB:
            w1q = sbB("w1q", [128, NDT * FF], BF16)
            wsl = sbB("wsl", [128, 2 * 2048])
            wqt = sbB("wqt", [128, 2048])
            onwb = sbB("onwb_s", [128, D])
            onbb = sbB("onbb_s", [128, D])
            att = sbB("attB", [128, D])
            att2 = sbB("att2", [128, D])
            fdq = sbB("fdq", [128, D])
            xot = sbB("xotB", [128, 2 * D])
            spl = sbB("spl", [128, 2 * D])
            w2s = sbB("w2s", [128, 2 * D], BF16)
            sl = sbB("sl", [128, NFT * FCW], BF16)
            hsb = sbB("hsb", [128, FCW + 2])
            accv = sbB("accv", [128, FCW])
            accn = sbB("accn", [128, FCW])
            accb = sbB("accb", [128, FCW], BF16)
            acct = sbB("acct", [128, FCW], BF16)
            hq = sbB("hq", [128, 2 * FCW], BF16)
            uq = sbB("uq", [128, FCW])
            zi4 = sbB("zi4", [128, FCW], I32)
            s2bw = sbB("s2bw", [128, FCW])
            srow = sbB("srow", [1, 256], BF16)
            if dbg and T <= 512:
                dbgfd_s = sbB("dbgfd_s", [128, D])
                dbgal_s = sbB("dbgal_s", [128, D])
            pb4 = esB.enter_context(nc.psum_tensor("pb4", [128, 2048], F32))

            @blk.gpsimd
            def _(g):
                g.wait_ge(s_dv, P35_DONE)
                dma_barrier(g)
                g.dma_start(out=onwb[:, :], in_=onw_h[:, :]).then_inc(s_gp, 16)
                c_gp.inc(16)
                g.dma_start(out=onbb[:, :], in_=onb_h[:, :]).then_inc(s_gp, 16)
                c_gp.inc(16)
            PW_ON = c_gp.v

            PW_LD, PW_RS = {}, {}
            pw_list = ([(0, s_) for s_ in range(2 * NDT)] +
                       [(1, s_) for s_ in range(NFT)])
            whs = {0: (w1t_h, 2048), 1: (w2t_h, D)}
            for li, (wi, s_) in enumerate(pw_list):
                wh, slw = whs[wi]
                par = li % 2

                eng_a = blk.gpsimd if li % 2 == 0 else blk.sync

                @eng_a
                def _(g, li=li, wi=wi, s_=s_, par=par, wh=wh, slw=slw):
                    if li <= 1:
                        g.wait_ge(s_dv, P35_DONE)
                        dma_barrier(g)
                    if li >= 2:
                        g.wait_ge(s_dv, PW_RS[li - 2])
                    sem, ctr = (s_gx, c_gx) if li % 2 == 0 else (s_gx2, c_gx2)
                    g.dma_start(out=wsl[:, par * 2048:par * 2048 + slw],
                                in_=wh[:, s_ * slw:(s_ + 1) * slw]).then_inc(sem, 16)
                    ctr.inc(16)
                    PW_LD[li] = ctr.v

                @blk.vector
                def _(v, li=li, wi=wi, s_=s_, par=par, slw=slw):
                    v.wait_ge(s_gx if li % 2 == 0 else s_gx2, PW_LD[li])
                    if s_ == 0:
                        v.reduce_sum(out=colb[:, CB_T0 + wi:CB_T0 + wi + 1],
                                     in_=wsl[:, par * 2048:par * 2048 + slw],
                                     axis=AX.X,
                                     apply_absolute_value=True).then_inc(s_dv, 1)
                    else:
                        v.reduce_sum(out=colb[:, CB_T1:CB_T1 + 1],
                                     in_=wsl[:, par * 2048:par * 2048 + slw],
                                     axis=AX.X, apply_absolute_value=True)
                        v.tensor_copy(out=colb[:, CB_T1 + 2:CB_T1 + 3],
                                      in_=colb[:, CB_M1:CB_M1 + 1])
                        v.tensor_tensor(out=colb[:, CB_T0 + wi:CB_T0 + wi + 1],
                                        in0=colb[:, CB_T0 + wi:CB_T0 + wi + 1],
                                        in1=colb[:, CB_T1:CB_T1 + 1],
                                        op=OP.add).then_inc(s_dv, 1)
                    c_dv.inc()
                    PW_RS[li] = c_dv.v
            PW_A = c_dv.v

            @blk.gpsimd
            def _(g):
                g.wait_ge(s_dv, PW_A)
                g.dma_start(out=scr_h[512:640],
                            in_=colb[:, CB_T0:CB_T0 + 1]).then_inc(s_gp, 16)
                c_gp.inc(16)
                g.dma_start(out=scr_h[640:768],
                            in_=colb[:, CB_T0 + 1:CB_T0 + 2]).then_inc(s_gp, 16)
                c_gp.inc(16)
                g.wait_ge(s_gp, c_gp.v)
                g.dma_start(out=scst[0:1, 64:192],
                            in_=scr_h[512:640]).then_inc(s_gp, 16)
                c_gp.inc(16)
                g.dma_start(out=scst[0:1, 192:320],
                            in_=scr_h[640:768]).then_inc(s_gp, 16)
                c_gp.inc(16)
            PW_ROWS = c_gp.v

            @blk.vector
            def _(v):
                v.wait_ge(s_gp, PW_ROWS)
                v.reduce_sum(out=scst[0:1, 56:57], in_=scst[0:1, 64:192], axis=AX.X)
                v.reduce_sum(out=scst[0:1, 57:58], in_=scst[0:1, 192:320], axis=AX.X)
                v.tensor_copy(out=scst[0:1, 60:61], in_=scst[0:1, 64:65])
                v.tensor_scalar(out=scst[0:1, 56:58], in0=scst[0:1, 56:58],
                                scalar1=1.0 / (FF * D), scalar2=None, op0=OP.mult)
                v.tensor_copy(out=scst[0:1, 60:61], in_=scst[0:1, 64:65])
                v.tensor_scalar(out=scst[0:1, 56:58], in0=scst[0:1, 56:58],
                                scalar1=1e-5, scalar2=None, op0=OP.max)
                v.tensor_copy(out=scst[0:1, 60:61], in_=scst[0:1, 64:65])
                v.reciprocal(out=scst[0:1, 58:60], in_=scst[0:1, 56:58])
                v.tensor_copy(out=scstb[0:1, 12:14], in_=scst[0:1, 56:58])
                v.tensor_copy(out=scstb[0:1, 14:16],
                              in_=scst[0:1, 58:60]).then_inc(s_dv, 1)
            PW_SC = c_dv.inc()

            @blk.tensor
            def _(t):
                t.wait_ge(s_dv, PW_SC)
                t.matmul(pb4[:, 0:1], ones_r[0:1, :], scstb[0:1, 12:13], start=True,
                         stop=True)
                t.matmul(pb4[:, 1:2], ones_r[0:1, :], scstb[0:1, 13:14], start=True,
                         stop=True)
                t.matmul(pb4[:, 2:3], ones_r[0:1, :], scstb[0:1, 14:15], start=True,
                         stop=True)
                t.matmul(pb4[:, 3:4], ones_r[0:1, :], scstb[0:1, 15:16], start=True,
                         stop=True).then_inc(s_pe, 1)
            PW_REP = c_pe.inc()

            @blk.vector
            def _(v):
                v.wait_ge(s_pe, PW_REP)
                v.tensor_copy(out=colb[:, CB_M1BAR:CB_M1BAR + 1], in_=pb4[:, 0:1])
                v.tensor_scalar(out=colb[:, CB_M2B127:CB_M2B127 + 1], in0=pb4[:, 1:2],
                                scalar1=1.0 / 127.0, scalar2=None, op0=OP.mult)
                v.tensor_copy(out=colb[:, CB_T0:CB_T0 + 1], in_=pb4[:, 2:3])
                v.tensor_copy(out=colb[:, CB_T1 + 1:CB_T1 + 2], in_=pb4[:, 3:4])
                v.tensor_scalar(out=cwc[:, :], in0=cwc[:, :],
                                scalar1=colb[:, CB_M1BAR:CB_M1BAR + 1], scalar2=None,
                                op0=OP.mult).then_inc(s_dv, 1)
            PW_COLS = c_dv.inc()

            PW_QLD, PW_QT, PW_QST = {}, {}, {}
            for li, (wi, s_) in enumerate(pw_list):
                wh, slw = whs[wi]
                par = li % 2

                eng_b = blk.gpsimd if li % 2 == 0 else blk.sync

                @eng_b
                def _(g, li=li, wi=wi, s_=s_, par=par, wh=wh, slw=slw):
                    if li <= 1:
                        g.wait_ge(s_dv, PW_COLS)
                    if li >= 2:
                        g.wait_ge(s_dv, PW_QT[li - 2])
                    sem, ctr = (s_gx, c_gx) if li % 2 == 0 else (s_gx2, c_gx2)
                    g.dma_start(out=wsl[:, par * 2048:par * 2048 + slw],
                                in_=wh[:, s_ * slw:(s_ + 1) * slw]).then_inc(sem, 16)
                    ctr.inc(16)
                    PW_QLD[li] = ctr.v

                @blk.vector
                def _(v, li=li, wi=wi, s_=s_, par=par, slw=slw):
                    v.wait_ge(s_gx if li % 2 == 0 else s_gx2, PW_QLD[li])
                    if wi == 1 and s_ >= 2:
                        v.wait_ge(s_gp, PW_QST[s_ - 2])

                    scol = colb[:, CB_T0 + wi:CB_T0 + wi + 1] if wi == 0 else \
                        colb[:, CB_T1 + 1:CB_T1 + 2]
                    src = wsl[:, par * 2048:par * 2048 + slw]
                    v.tensor_scalar(out=wqt[:, 0:slw], in0=src, scalar1=scol,
                                    scalar2=0.5, op0=OP.mult, op1=OP.is_ge)
                    v.tensor_scalar(out=pb4[:, 0:slw], in0=src, scalar1=scol,
                                    scalar2=-0.5, op0=OP.mult, op1=OP.is_ge)
                    v.tensor_copy(out=colpad[:, 2:3],
                                  in_=colb[:, CB_M1:CB_M1 + 1])
                    dst = w1q[:, s_ * 2048:(s_ + 1) * 2048] if wi == 0 else \
                        w2s[:, (s_ % 2) * D:(s_ % 2) * D + D]
                    v.scalar_tensor_tensor(out=dst, in0=wqt[:, 0:slw], scalar=-1.0,
                                           op0=OP.add, in1=pb4[:, 0:slw],
                                           op1=OP.add).then_inc(s_dv, 1)
                    c_dv.inc()
                    PW_QT[li] = c_dv.v

                if wi == 1:
                    @blk.gpsimd
                    def _(g, li=li, s_=s_):
                        g.wait_ge(s_dv, PW_QT[li])
                        g.wait_ge(s_gp, c_gp.v)
                        g.dma_start(out=w2qh[:, s_ * D:(s_ + 1) * D],
                                    in_=w2s[:, (s_ % 2) * D:(s_ % 2) * D + D]
                                    ).then_inc(s_gp, 16)
                        c_gp.inc(16)
                        PW_QST[s_] = c_gp.v
            PW_DONE_GP = c_gp.v

            P4_W2, P4_M1, P4_CV, P4_SL, P4_S2A, P4_S2B = {}, {}, {}, {}, {}, {}
            P4_S2C, P4_HQ, P4_M2, P4_FD, P4_LN, P4_AD, P4_OUT = ({} for _ in range(7))
            for fc in range(NFC):
                for ft in range(NFT):
                    phb = ft % 2

                    @blk.tensor
                    def _(t, fc=fc, ft=ft, phb=phb):
                        if fc == 0 and ft == 0:
                            t.wait_ge(s_gp, PW_DONE_GP)
                            t.wait_ge(s_dv, P2_TRXC[NTT - 1])
                        key = (fc, ft - 2) if ft >= 2 else \
                            ((fc - 1, NFT - 2 + ft) if fc > 0 else None)
                        if key is not None:
                            t.wait_ge(s_dv, P4_CV[key])
                        for dt in range(NDT):
                            inst = t.matmul(
                                pb4[:, phb * 512:phb * 512 + FCW + 2],
                                w1q[:, dt * FF + ft * 128:dt * FF + (ft + 1) * 128],
                                xnq[:, dt * (T + 2) + fc * FCW:
                                    dt * (T + 2) + fc * FCW + FCW + 2],
                                start=(dt == 0), stop=(dt == NDT - 1))
                        inst.then_inc(s_pe, 1)
                        c_pe.inc()
                        P4_M1[(fc, ft)] = c_pe.v

                    @blk.vector
                    def _(v, fc=fc, ft=ft, phb=phb):
                        v.wait_ge(s_pe, P4_M1[(fc, ft)])
                        if ft >= 1:
                            v.wait_ge(s_ac, P4_S2A[(fc, ft - 1)])
                        elif fc > 0:
                            v.wait_ge(s_ac, P4_S2A[(fc - 1, NFT - 1)])
                        v.tensor_tensor(out=hsb[:, :],
                                        in0=pb4[:, phb * 512:phb * 512 + FCW + 2],
                                        in1=invqsb[:, fc * FCW:fc * FCW + FCW + 2],
                                        op=OP.mult).then_inc(s_dv, 1)
                        c_dv.inc()
                        P4_CV[(fc, ft)] = c_dv.v
                        w0 = cwc[:, 3 * ft:3 * ft + 1]
                        w1c = cwc[:, 3 * ft + 1:3 * ft + 2]
                        w2c = cwc[:, 3 * ft + 2:3 * ft + 3]
                        v.tensor_scalar(out=uq[:, :], in0=hsb[:, 0:FCW], scalar1=w0,
                                        scalar2=None, op0=OP.mult)
                        v.scalar_tensor_tensor(out=uq[:, :], in0=hsb[:, 1:FCW + 1],
                                               scalar=w1c, op0=OP.mult, in1=uq[:, :],
                                               op1=OP.add)
                        v.scalar_tensor_tensor(out=uq[:, :], in0=hsb[:, 2:FCW + 2],
                                               scalar=w2c, op0=OP.mult, in1=uq[:, :],
                                               op1=OP.add).then_inc(s_dv, 1)
                        c_dv.inc()
                        P4_SL[(fc, ft)] = c_dv.v

                    @blk.scalar
                    def _(s, fc=fc, ft=ft):
                        s.wait_ge(s_dv, P4_SL[(fc, ft)])
                        s.activation(out=sl[:, ft * FCW:(ft + 1) * FCW], in_=uq[:, :],
                                     func=AF.Silu, bias=cbc[:, ft:ft + 1],
                                     scale=1.0).then_inc(s_ac, 1)
                        c_ac.inc()
                        P4_S2A[(fc, ft)] = c_ac.v

                    @blk.vector
                    def _(v, fc=fc, ft=ft):
                        v.wait_ge(s_ac, P4_S2A[(fc, ft)])
                        if ft == 0:
                            v.tensor_copy(out=accv[:, :], in_=sl[:, 0:FCW])
                            v.tensor_copy(out=accn[:, :],
                                          in_=sl[:, 0:FCW]).then_inc(s_dv, 1)
                        else:
                            v.tensor_tensor(out=accv[:, :], in0=accv[:, :],
                                            in1=sl[:, ft * FCW:(ft + 1) * FCW],
                                            op=OP.max)
                            v.tensor_tensor(out=accn[:, :], in0=accn[:, :],
                                            in1=sl[:, ft * FCW:(ft + 1) * FCW],
                                            op=OP.min).then_inc(s_dv, 1)
                        c_dv.inc()
                        P4_S2B[(fc, ft)] = c_dv.v

                @blk.vector
                def _(v, fc=fc):
                    v.wait_ge(s_dv, P4_S2B[(fc, NFT - 1)])
                    v.tensor_scalar(out=accn[:, :], in0=accn[:, :], scalar1=-1.0,
                                    scalar2=None, op0=OP.mult)
                    v.tensor_tensor(out=accv[:, :], in0=accv[:, :], in1=accn[:, :],
                                    op=OP.max)
                    v.tensor_copy(out=accb[:, :], in_=accv[:, :]).then_inc(s_dv, 1)
                    c_dv.inc()
                    P4_S2C[(fc, 0)] = c_dv.v

                @blk.scalar
                def _(s, fc=fc):
                    s.wait_ge(s_dv, P4_S2C[(fc, 0)])
                    s.dma_start_transpose(out=acct[:, 0:128],
                                          in_=accb[:, 0:128]).then_inc(s_tr, 16)
                    c_tr.inc(16)
                    s.dma_start_transpose(out=acct[:, 128:256],
                                          in_=accb[:, 128:256]).then_inc(s_tr, 16)
                    c_tr.inc(16)
                    P4_S2C[(fc, "tr")] = c_tr.v

                @blk.vector
                def _(v, fc=fc):
                    v.wait_ge(s_tr, P4_S2C[(fc, "tr")])
                    v.reduce_max(out=colb[:, CB_T0 + 2:CB_T0 + 3], in_=acct[:, 0:128],
                                 axis=AX.X)
                    v.reduce_max(out=colb[:, CB_T0 + 3:CB_T0 + 4],
                                 in_=acct[:, 128:256], axis=AX.X)
                    v.tensor_copy(out=colpad[:, 2:3],
                                  in_=colb[:, CB_M1:CB_M1 + 1])
                    v.tensor_scalar(out=colb[:, CB_T0 + 2:CB_T0 + 4],
                                    in0=colb[:, CB_T0 + 2:CB_T0 + 4], scalar1=1e-5,
                                    scalar2=None, op0=OP.max)
                    v.tensor_copy(out=colpad[:, 2:3],
                                  in_=colb[:, CB_M1:CB_M1 + 1])
                    v.tensor_scalar(out=colb[:, CB_G2W + 2 * fc:CB_G2W + 2 * fc + 2],
                                    in0=colb[:, CB_T0 + 2:CB_T0 + 4],
                                    scalar1=colb[:, CB_M2B127:CB_M2B127 + 1],
                                    scalar2=None, op0=OP.mult)
                    v.reciprocal(out=colb[:, CB_T1 + 2:CB_T1 + 4],
                                 in_=colb[:, CB_T0 + 2:CB_T0 + 4])
                    v.tensor_copy(out=colpad[:, 2:3],
                                  in_=colb[:, CB_M1:CB_M1 + 1])
                    v.tensor_scalar(out=colb[:, CB_T1 + 2:CB_T1 + 4],
                                    in0=colb[:, CB_T1 + 2:CB_T1 + 4], scalar1=127.0,
                                    scalar2=None, op0=OP.mult)
                    v.tensor_copy(out=colpad[:, 2:3],
                                  in_=colb[:, CB_M1:CB_M1 + 1])
                    v.tensor_copy(out=colpad[:, 0:2],
                                  in_=colb[:, CB_T1 + 2:CB_T1 + 4]).then_inc(s_dv, 1)
                    c_dv.inc()
                    P4_S2C[(fc, 1)] = c_dv.v

                @blk.scalar
                def _(s, fc=fc):
                    s.wait_ge(s_dv, P4_S2C[(fc, 1)])
                    s.dma_start_transpose(out=rowt[:, :],
                                          in_=colpad[:, :]).then_inc(s_tr, 16)
                    c_tr.inc(16)
                    P4_S2C[(fc, "t2")] = c_tr.v

                @blk.gpsimd
                def _(g, fc=fc):
                    g.wait_ge(s_tr, P4_S2C[(fc, "t2")])
                    g.dma_start(out=srow[0:1, :],
                                in_=rowt[0:2, 0:128]).then_inc(s_gp, 16)
                    c_gp.inc(16)
                    P4_S2C[(fc, "rw")] = c_gp.v

                @blk.tensor
                def _(t, fc=fc):
                    t.wait_ge(s_gp, P4_S2C[(fc, "rw")])
                    t.matmul(pb4[:, 1024:1152], ones_r[0:1, :], srow[0:1, 0:128],
                             start=True, stop=True)
                    t.matmul(pb4[:, 1152:1280], ones_r[0:1, :], srow[0:1, 128:256],
                             start=True, stop=True).then_inc(s_pe, 1)
                    c_pe.inc()
                    P4_S2C[(fc, 2)] = c_pe.v

                @blk.vector
                def _(v, fc=fc):
                    v.wait_ge(s_pe, P4_S2C[(fc, 2)])
                    v.tensor_copy(out=s2bw[:, 0:256],
                                  in_=pb4[:, 1024:1280]).then_inc(s_dv, 1)
                    c_dv.inc()
                    P4_S2C[(fc, 3)] = c_dv.v

                for ft in range(NFT):
                    @blk.sync
                    def _(sy, fc=fc, ft=ft):
                        if fc == 0 and ft == 0:
                            sy.wait_ge(s_gp, PW_DONE_GP)
                        key = (fc, ft - 2) if ft >= 2 else \
                            ((fc - 1, NFT - 2 + ft) if fc > 0 else None)
                        if key is not None:
                            sy.wait_ge(s_pe, P4_M2[key])
                        sem, ctr = (s_in, c_in) if ft % 2 == 0 else (s_i2, c_i2)
                        sy.dma_start(out=w2s[:, (ft % 2) * D:(ft % 2 + 1) * D],
                                     in_=w2qh[:, ft * D:(ft + 1) * D]).then_inc(sem, 16)
                        ctr.inc(16)
                        P4_W2[(fc, ft)] = ctr.v

                    @blk.vector
                    def _(v, fc=fc, ft=ft):
                        if ft == 0:
                            v.wait_ge(s_dv, P4_S2C[(fc, 3)])
                        if ft >= 2:
                            v.wait_ge(s_pe, P4_M2[(fc, ft - 2)])
                        v.tensor_tensor(out=uq[:, :],
                                        in0=sl[:, ft * FCW:(ft + 1) * FCW],
                                        in1=s2bw[:, 0:FCW], op=OP.mult)
                        v.tensor_scalar(out=uq[:, :], in0=uq[:, :], scalar1=128.5,
                                        scalar2=None, op0=OP.add)
                        v.tensor_copy(out=zi4[:, :], in_=uq[:, :])
                        v.tensor_copy(out=accv[:, :], in_=zi4[:, :])
                        v.tensor_tensor(out=accb[:, :], in0=accv[:, :],
                                        in1=uq[:, :], op=OP.is_le)
                        v.tensor_tensor(out=accv[:, :], in0=accv[:, :],
                                        in1=accb[:, :], op=OP.add)
                        v.tensor_scalar(out=hq[:, (ft % 2) * FCW:(ft % 2 + 1) * FCW],
                                        in0=accv[:, :], scalar1=-129.0, scalar2=127.0,
                                        op0=OP.add, op1=OP.min).then_inc(s_dv, 1)
                        c_dv.inc()
                        P4_HQ[(fc, ft)] = c_dv.v

                    @blk.tensor
                    def _(t, fc=fc, ft=ft):
                        t.wait_ge(s_dv, P4_HQ[(fc, ft)])
                        t.wait_ge(s_in if ft % 2 == 0 else s_i2, P4_W2[(fc, ft)])
                        if ft == 0 and fc > 0:
                            t.wait_ge(s_ac, P4_FD[(fc - 1) * 2 + 1])
                        for isub in range(2):
                            for ec in range(2):
                                inst = t.matmul(
                                    pacc[:, (isub * 2 + ec) * 512:
                                         (isub * 2 + ec + 1) * 512],
                                    hq[:, (ft % 2) * FCW + isub * 128:
                                       (ft % 2) * FCW + (isub + 1) * 128],
                                    w2s[:, (ft % 2) * D + ec * 512:
                                        (ft % 2) * D + (ec + 1) * 512],
                                    start=(ft == 0), stop=(ft == NFT - 1))
                        inst.then_inc(s_pe, 1)
                        c_pe.inc()
                        P4_M2[(fc, ft)] = c_pe.v

                if dbg and fc == 0:
                    @blk.sync
                    def _(sy):
                        sy.wait_ge(s_pe, P4_M2[(0, NFT - 1)])
                        sy.dma_start(out=dbg_sl, in_=sl[:, :]).then_inc(s_st, 16)
                        c_st.inc(16)
                        sy.dma_start(out=dbg_w1q, in_=w1q[:, :]).then_inc(s_st, 16)
                        c_st.inc(16)
                        sy.dma_start(out=dbg_s2, in_=s2bw[:, :]).then_inc(s_st, 16)
                        c_st.inc(16)
                        sy.dma_start(out=dbg_hq, in_=hq[:, :]).then_inc(s_st, 16)
                        c_st.inc(16)

                for isub in range(2):
                    tix = fc * 2 + isub

                    @blk.gpsimd
                    def _(g, fc=fc, isub=isub, tix=tix):
                        if tix == 0:
                            g.wait_ge(s_in, P3_SP_LAST)
                            dma_barrier(g)
                        if tix >= 2:
                            g.wait_ge(s_dv, P4_AD[tix - 2])
                        sem, ctr = (s_gx, c_gx) if tix % 2 == 0 else (s_gx2, c_gx2)
                        g.dma_start(out=spl[:, (tix % 2) * D:(tix % 2 + 1) * D],
                                    in_=spill_h[tix * 128:(tix + 1) * 128, :]
                                    ).then_inc(sem, 16)
                        ctr.inc(16)
                        g.dma_start(out=xot[:, (tix % 2) * D:(tix % 2 + 1) * D],
                                    in_=xow_h[tix * 128:(tix + 1) * 128, :]
                                    ).then_inc(sem, 16)
                        ctr.inc(16)
                        P4_LN[tix] = ctr.v

                    @blk.scalar
                    def _(s, fc=fc, isub=isub, tix=tix):
                        s.wait_ge(s_pe, P4_M2[(fc, NFT - 1)])
                        s.wait_ge(s_gx if tix % 2 == 0 else s_gx2, P4_LN[tix])
                        if tix >= 1:
                            s.wait_ge(s_dv, P4_AD[tix - 1])
                            s.wait_ge(s_st, P4_OUT[tix - 1])
                        s.activation(out=fdq[:, :],
                                     in_=pacc[:, isub * 1024:(isub + 1) * 1024],
                                     func=AF.Copy, bias=0.0, scale=1.0)
                        s.activation(out=att[:, :],
                                     in_=spl[:, (tix % 2) * D:(tix % 2 + 1) * D],
                                     func=AF.Identity,
                                     bias=colb[:, CB_LNB + tix:CB_LNB + tix + 1],
                                     scale=colb[:, CB_LNA + tix:CB_LNA + tix + 1]
                                     ).then_inc(s_ac, 1)
                        c_ac.inc()
                        P4_FD[tix] = c_ac.v

                    @blk.vector
                    def _(v, fc=fc, isub=isub, tix=tix):
                        v.wait_ge(s_ac, P4_FD[tix])
                        if tix >= 1:
                            v.wait_ge(s_st, P4_OUT[tix - 1])
                        g2w = colb[:, CB_G2W + 2 * fc + isub:
                                   CB_G2W + 2 * fc + isub + 1]
                        v.tensor_scalar(out=fdq[:, :], in0=fdq[:, :], scalar1=g2w,
                                        scalar2=None, op0=OP.mult)
                        if dbg and T <= 512 and tix == 0:
                            v.tensor_copy(out=dbgfd_s[:, :], in_=fdq[:, :])
                            v.tensor_copy(out=dbgal_s[:, :], in_=att[:, :])
                        v.tensor_tensor(out=att2[:, :], in0=att[:, :], in1=onwb[:, :],
                                        op=OP.mult)
                        v.tensor_tensor(out=att2[:, :], in0=att2[:, :], in1=onbb[:, :],
                                        op=OP.add)
                        v.tensor_tensor(out=att2[:, :], in0=att2[:, :], in1=fdq[:, :],
                                        op=OP.add)
                        v.tensor_tensor(out=att[:, :], in0=att2[:, :],
                                        in1=xot[:, (tix % 2) * D:(tix % 2 + 1) * D],
                                        op=OP.add).then_inc(s_dv, 1)
                        c_dv.inc()
                        P4_AD[tix] = c_dv.v

                    @blk.sync
                    def _(sy, tix=tix):
                        sy.wait_ge(s_dv, P4_AD[tix])
                        sy.dma_start(out=y_h[tix * 128:(tix + 1) * 128, :],
                                     in_=att[:, :]).then_inc(s_st, 16)
                        c_st.inc(16)
                        P4_OUT[tix] = c_st.v

            if dbg and T <= 512:
                @blk.sync
                def _(sy):
                    sy.wait_ge(s_dv, P4_AD[0] if 0 in P4_AD else c_dv.v)
                    sy.dma_start(out=dbg_fd, in_=dbgfd_s[:, :]).then_inc(s_st, 16)
                    c_st.inc(16)
                    sy.dma_start(out=dbg_al, in_=dbgal_s[:, :]).then_inc(s_st, 16)
                    c_st.inc(16)
            if dbg:
                @blk.sync
                def _(sy):
                    sy.wait_ge(s_dv, c_dv.v)
                    sy.dma_start(out=dbg_scst, in_=scst[0:1, :]).then_inc(s_st, 16)
                    c_st.inc(16)
                    sy.dma_start(out=dbg_cb2, in_=colb[:, :]).then_inc(s_st, 16)
                    c_st.inc(16)

            @blk.sync
            def _(sy):
                sy.wait_ge(s_st, c_st.v)
    return nc


def _tile_pd(a, p=128):
    n = a.shape[0] // p
    return np.ascontiguousarray(
        a.reshape(n, p, a.shape[1]).transpose(1, 0, 2).reshape(p, n * a.shape[1]))


def kernel(L0_attn_x, L0_attn_Wq, L0_attn_Wkv, L0_attn_Wo, L0_attn_norm_w,
           L0_attn_norm_b, L0_attn_out_norm_w, L0_attn_out_norm_b, L0_ffn_w1,
           L0_ffn_conv_w, L0_ffn_conv_b, L0_ffn_w2):
    x = np.asarray(L0_attn_x, dtype=np.float32)
    B, NB, _ = x.shape
    T = NB // 4
    nc = build(NB, T)

    f32 = np.float32
    wq_s = _tile_pd(np.asarray(L0_attn_Wq, f32))
    wkv_s = _tile_pd(np.asarray(L0_attn_Wkv, f32))
    wo_s = _tile_pd(np.asarray(L0_attn_Wo, f32))
    w1t_s = _tile_pd(np.ascontiguousarray(np.asarray(L0_ffn_w1, f32).T))
    w2t_s = _tile_pd(np.ascontiguousarray(np.asarray(L0_ffn_w2, f32).T))
    cw_s = _tile_pd(np.ascontiguousarray(np.asarray(L0_ffn_conv_w, f32)[:, 0, :]))
    cb_s = _tile_pd(np.asarray(L0_ffn_conv_b, f32).reshape(-1, 1))
    nw_s = _tile_pd(np.asarray(L0_attn_norm_w, f32).reshape(-1, 1))
    nb_s = _tile_pd(np.asarray(L0_attn_norm_b, f32).reshape(-1, 1))
    nwb_s = np.ascontiguousarray(np.broadcast_to(np.asarray(L0_attn_norm_w, f32), (128, 1024)))
    nbb_s = np.ascontiguousarray(np.broadcast_to(np.asarray(L0_attn_norm_b, f32), (128, 1024)))
    onw_s = np.ascontiguousarray(np.broadcast_to(np.asarray(L0_attn_out_norm_w, f32), (128, 1024)))
    onb_s = np.ascontiguousarray(np.broadcast_to(np.asarray(L0_attn_out_norm_b, f32), (128, 1024)))

    in_maps = []
    for c in range(8):
        b, qo = c // 4, c % 4
        r0 = qo * T
        xb_ = x[b]
        rot = np.concatenate([xb_[r0:], xb_[:r0]], axis=0)
        im = {
            "xT": np.ascontiguousarray(rot.T),
            "xown": np.ascontiguousarray(xb_[r0:r0 + T]),
            "wq": wq_s, "wkv": wkv_s, "wo": wo_s,
            "w1t": w1t_s, "w2t": w2t_s,
            "convw": cw_s, "convb": cb_s, "nw": nw_s, "nb": nb_s,
            "nwb": nwb_s, "nbb": nbb_s, "onwb": onw_s, "onbb": onb_s,
            "hmask": np.ascontiguousarray(
                np.broadcast_to(np.array([[1.0 if qo > 0 else 0.0,
                                           1.0 if qo < 3 else 0.0]], f32),
                                (128, 2))),
        }
        in_maps.append(im)

    res = run_bass_kernel_spmd(nc, in_maps, core_ids=list(range(8)))
    out = np.empty((B, NB, 1024), dtype=np.float32)
    for c in range(8):
        b, qo = c // 4, c % 4
        out[b, qo * T:(qo + 1) * T] = res.results[c]["y"]
    return out
